# version 12
# baseline (speedup 1.0000x reference)
import contextlib
import numpy as np
import concourse.bass as bass
import concourse.mybir as mybir
from concourse.bass_utils import run_bass_kernel_spmd

F32 = mybir.dt.float32
BF16 = mybir.dt.bfloat16
AF = mybir.ActivationFunctionType
ALU = mybir.AluOpType

B, T, D = 8, 2048, 1024
H, HD = 16, 64
NKT = T // 128
NQC = T // 512
DT = D // 128
N_HEAVY = 1024
COOR_K = 9.0 / 8.0 * N_HEAVY
CAP = 30.0
N_IN, N_PROC, N_OUT, D_PV = 8192, 4096, 8192, 512

DEBUG_STAGE = None


class Ctx:
    def __init__(self):
        self.nc = bass.Bass("TRN2", target_bir_lowering=False)
        self.es = contextlib.ExitStack()
        self.block = None
        self.sems = {}
        self.cnt = {}
        self.guard = {}
        self.scq = []

    def new_sem(self, name):
        self.sems[name] = self.es.enter_context(self.nc.semaphore(name))
        self.cnt[name] = 0

    def inc(self, inst, sem, n=1):
        inst.then_inc(self.sems[sem], n)
        self.cnt[sem] += n
        return self.cnt[sem]

    def dma(self, eng, sem, out, in_):
        i = eng.dma_start(out=out, in_=in_)
        return self.inc(i, sem, 16)

    def wait(self, eng, sem, val):
        if val > 0:
            eng.wait_ge(self.sems[sem], val)

    def use(self, buf, sem, val):
        self.guard[buf] = [(sem, val)]

    def add_use(self, buf, sem, val):
        self.guard.setdefault(buf, []).append((sem, val))

    def wait_buf(self, eng, buf):
        for (s, v) in self.guard.get(buf, []):
            self.wait(eng, s, v)

    def reserve(self, sem, n=1):
        self.cnt[sem] += n
        return self.cnt[sem]

    def inc_to(self, inst, sem, n=1):
        inst.then_inc(self.sems[sem], n)

    def snap(self, buf):
        return list(self.guard.get(buf, []))

    def wait_list(self, eng, lst):
        for (s, v) in lst:
            self.wait(eng, s, v)

    def fence_to(self, eng, sem, inst, val):
        inst.then_inc(self.sems[sem], 1)
        eng.wait_ge(self.sems[sem], val)

    def sc(self, fn):
        self.scq.append(fn)

    def sc_flush(self):
        if self.scq:
            fns = self.scq
            self.scq = []
            self.block.scalar(lambda s: [f(s) for f in fns])

    def sbuf_epoch(self, extra=()):
        self.guard["SBUF"] = [(s, self.cnt[s]) for s in
                              ("tC", "vC", "sC", "gC", *extra)]

    def fence(self, eng, sem, inst):
        self.inc(inst, sem)
        self.wait(eng, sem, self.cnt[sem])


def build_program(debug_stage=None):
    c = Ctx()
    nc = c.nc
    es = c.es
    dram = {}

    def din(name, shape):
        dram[name] = nc.dram_tensor(name, list(shape), F32, kind="ExternalInput").ap()
        return dram[name]

    din("xT", (D, T))
    din("rope_cos", (128, T)); din("rope_sin", (128, T))
    din("onehot", (8, 1)); din("ones128", (128, 128)); din("blkones", (128, 128))
    din("ident", (128, 128)); din("ltri", (128, 128))
    for L in (0, 1):
        for n in ("wq", "wk", "wv", "wo"):
            din(f"a{L}_{n}T", (D, D))
    din("rt", (128, DT))
    din("gl", (128, DT)); din("gh", (128, DT))
    din("b2l", (128, DT)); din("b2h", (128, DT))
    din("b1l", (128, 4)); din("b1h", (128, 32))
    din("w1lT", (D, 512)); din("w2lT", (512, D))
    din("w1hT_blk", (32 * 1024, 128)); din("w2hT_blk", (8 * 4096, 128))
    din("ipT_blk", (64 * 1024, 128))
    din("piwT_sh", (N_IN, N_PROC // 8))
    din("pv", (N_PROC, D_PV))
    din("oiw_shT", (D_PV, N_OUT // 8))
    din("op_sh", (N_OUT // 8, D))
    din("ohrow", (128, 8))
    din("winm", (128, 512))

    out = nc.dram_tensor("outT", [D, T], F32, kind="ExternalOutput").ap()
    Hdram = nc.dram_tensor("Hdram", [4096, T], BF16).ap()

    ccd = {}
    ccd_h = {}
    for n, shape in (("irA", (128, 512)), ("irB", (128, 512)),
                     ("paA", (8, N_PROC)), ("paB", (8, N_PROC)),
                     ("oaA", (8, N_OUT)), ("oaB", (8, N_OUT)),
                     ("osA", (8, D)), ("osB", (8, D))):
        h = nc.dram_tensor(n, list(shape), F32)
        ccd_h[n] = h
        ccd[n] = h.ap()

    X = es.enter_context(nc.sbuf_tensor("X", [128, DT * T], F32))
    MREP = es.enter_context(nc.sbuf_tensor("MREP", [128, T], F32))
    IREP = es.enter_context(nc.sbuf_tensor("IREP", [128, T], F32))
    SCB = es.enter_context(nc.sbuf_tensor("SCB", [128, T], BF16))
    ONES = es.enter_context(nc.sbuf_tensor("ONES", [128, 128], BF16))
    BLK = es.enter_context(nc.sbuf_tensor("BLK", [128, 128], BF16))
    BLKF = es.enter_context(nc.sbuf_tensor("BLKF", [128, 128], F32))
    ONE1 = es.enter_context(nc.sbuf_tensor("ONE1", [1, 64], BF16))
    ONE1F = es.enter_context(nc.sbuf_tensor("ONE1F", [1, 64], F32))
    ONESF = es.enter_context(nc.sbuf_tensor("ONESF", [128, 128], F32))
    IDN = es.enter_context(nc.sbuf_tensor("IDN", [128, 128], F32))
    LTRI = es.enter_context(nc.sbuf_tensor("LTRI", [128, 128], F32))
    SMALL = es.enter_context(nc.sbuf_tensor("SMALL", [128, 96], F32))
    SIG = es.enter_context(nc.sbuf_tensor("SIG", [128, 16], F32))
    SV = es.enter_context(nc.sbuf_tensor("SV", [128, 128], F32))

    def ap_x(t, c0=0, c1=T):
        return X[:, t * T + c0: t * T + c1]

    with nc.Block() as block:
        c.block = block
        for s in ["qX", "qX1", "qX2", "qX3", "qC", "qTAB", "qWV", "qWQK0",
                  "qWQK1", "qSH", "qMV", "qWO0", "qWO1", "qF0", "qF1", "qF2",
                  "qF2b", "qF3", "tC", "vC", "sC", "gC"]:
            c.new_sem(s)

        def loads(g):
            c.dma(g, "qC", ONES[:], dram["ones128"])
            c.dma(g, "qC", BLK[:], dram["blkones"])
            c.dma(g, "qC", ONE1[:], dram["ones128"][0:1, 0:64])
            c.dma(g, "qC", ONE1F[:], dram["ones128"][0:1, 0:64])
            c.dma(g, "qC", SMALL[:, 0:8], dram["rt"])
            c.dma(g, "qC", SMALL[:, 8:16], dram["gl"])
            c.dma(g, "qC", SMALL[:, 16:24], dram["gh"])
            c.dma(g, "qC", SMALL[:, 24:32], dram["b2l"])
            c.dma(g, "qC", SMALL[:, 32:40], dram["b2h"])
            c.dma(g, "qC", SMALL[:, 40:44], dram["b1l"])
            c.dma(g, "qC", SMALL[:, 44:76], dram["b1h"])
            c.dma(g, "qC", SMALL[0:8, 76:77], dram["onehot"])
            c.dma(g, "qC", ONESF[:], dram["ones128"])
            c.dma(g, "qC", BLKF[:], dram["blkones"])
            c.dma(g, "qC", IDN[:], dram["ident"])
            c.dma(g, "qC", LTRI[:], dram["ltri"])
        block.gpsimd(loads)
        CONST_READY = c.cnt["qC"]

        def loadx(s):
            for t in range(DT):
                c.dma(s, "qX", ap_x(t), dram["xT"][t * 128:(t + 1) * 128, :])
        block.sync(loadx)
        X_READY = c.cnt["qX"]

        def vconsts(v):
            v.memset(SV[:, 0:1], 1e-5)
            i = v.memset(SV[:, 1:2], 1.0)
            c.inc(i, "vC")
        block.vector(vconsts)
        EPS_READY = c.cnt["vC"]

        def layernorm(tag, LNX, first=False):
            def ap_lnx(t, c0=0, c1=T):
                return LNX[:, t * T + c0: t * T + c1]

            ps = contextlib.ExitStack()
            ps_sx = ps.enter_context(nc.psum_tensor(f"sx_{tag}", [128, T], F32))
            ps_s2 = ps.enter_context(nc.psum_tensor(f"s2_{tag}", [128, T], F32))

            pre_v = c.cnt["vC"]
            lnx_w = c.snap("LNX")
            sbuf_w = c.snap("SBUF")

            def vcast(v):
                c.wait(v, "vC", pre_v)
                if first:
                    c.wait(v, "qX", X_READY)
                c.wait_buf(v, "LNX")
                c.wait_buf(v, "SBUF")
                for t in range(DT // 2):
                    i = v.tensor_copy(ap_lnx(t), ap_x(t))
                c.inc(i, "vC")
            block.vector(vcast)
            vcast_done = c.cnt["vC"]

            scast_done = c.reserve("sC")
            def scast(sc, pre_v=pre_v, lnx_w=lnx_w, sbuf_w=sbuf_w, first=first):
                c.wait(sc, "vC", pre_v)
                if first:
                    c.wait(sc, "qX", X_READY)
                c.wait_list(sc, lnx_w)
                c.wait_list(sc, sbuf_w)
                for t in range(DT // 2, DT):
                    i = sc.activation(ap_lnx(t), ap_x(t), AF.Copy)
                c.inc_to(i, "sC")
            c.sc(scast)
            cast_done = (vcast_done, scast_done)

            def tsx(t_):
                c.wait(t_, "vC", cast_done[0])
                c.wait(t_, "sC", cast_done[1])
                if first:
                    c.wait(t_, "qC", CONST_READY)
                c.wait_buf(t_, "PSUM")
                for t in range(DT):
                    for q in range(NQC):
                        i = t_.matmul(ps_sx[:, q * 512:(q + 1) * 512], ONES[:],
                                      ap_lnx(t, q * 512, (q + 1) * 512),
                                      start=(t == 0), stop=(t == DT - 1))
                c.inc(i, "tC")
            block.tensor(tsx)

            for t in range(DT):
                sq_done = c.reserve("sC")
                scb_w = c.snap("SCB")
                def ssq(sc, t=t, scb_w=scb_w):
                    c.wait(sc, "vC", cast_done[0])
                    c.wait_list(sc, scb_w)
                    i = sc.activation(SCB[:, :], ap_lnx(t), AF.Square)
                    c.inc_to(i, "sC")
                c.sc(ssq)

                def ts2(t_, t=t, sq_done=sq_done):
                    c.wait(t_, "sC", sq_done)
                    for q in range(NQC):
                        i = t_.matmul(ps_s2[:, q * 512:(q + 1) * 512], ONES[:],
                                      SCB[:, q * 512:(q + 1) * 512],
                                      start=(t == 0), stop=(t == DT - 1))
                    c.inc(i, "tC")
                block.tensor(ts2)
                c.use("SCB", "tC", c.cnt["tC"])
            s2_done = c.cnt["tC"]

            def vstat(v):
                c.wait(v, "tC", s2_done)
                v.tensor_scalar(MREP[:, :], ps_sx[:, :], 1.0 / D, None, ALU.mult)
                i = v.tensor_scalar(IREP[:, :], ps_s2[:, :], 1.0 / D, None, ALU.mult)
                c.fence(v, "vC", i)
                c.wait_buf(v, "SCB")
                i = v.tensor_tensor(SCB[:, :], MREP[:, :], MREP[:, :], ALU.mult)
                c.fence(v, "vC", i)
                i = v.tensor_tensor(IREP[:, :], IREP[:, :], SCB[:, :], ALU.subtract)
                c.inc(i, "vC")
            block.vector(vstat)
            var_done = c.cnt["vC"]
            c.use("PSUM", "vC", var_done)

            f1 = c.reserve("sC")
            inv_done = c.reserve("sC")
            def sstat(sc, f1=f1):
                c.wait(sc, "vC", var_done)
                i = sc.activation(IREP[:, :], IREP[:, :], AF.Ln, bias=SV[:, 0:1])
                c.fence_to(sc, "sC", i, f1)
                i = sc.activation(IREP[:, :], IREP[:, :], AF.Exp, scale=-0.5)
                c.inc_to(i, "sC")
            c.sc(sstat)

            def vapply(v):
                c.wait(v, "sC", inv_done)
                i = v.tensor_tensor(MREP[:, :], MREP[:, :], IREP[:, :], ALU.mult)
                c.fence(v, "vC", i)
                for t in range(DT):
                    i = v.tensor_tensor(SCB[:, :], ap_x(t), IREP[:, :], ALU.mult)
                    c.fence(v, "vC", i)
                    i = v.tensor_tensor(ap_lnx(t), SCB[:, :], MREP[:, :], ALU.subtract)
                    c.fence(v, "vC", i)
            block.vector(vapply)
            ln_done = c.cnt["vC"]
            c.use("SCB", "vC", ln_done)
            ps.close()
            return ln_done

        def attention(L, LNX, ln_done):
            def ap_lnx(t, c0=0, c1=T):
                return LNX[:, t * T + c0: t * T + c1]

            wqT, wkT, wvT, woT = (dram[f"a{L}_{n}T"] for n in ("wq", "wk", "wv", "wo"))

            sco = contextlib.ExitStack()
            COS = sco.enter_context(nc.sbuf_tensor(f"COS{L}", [128, T], BF16))
            SIN = sco.enter_context(nc.sbuf_tensor(f"SIN{L}", [128, T], BF16))
            VT = sco.enter_context(nc.sbuf_tensor(f"VT{L}", [128, NKT * 520], BF16))
            OUTP = sco.enter_context(nc.sbuf_tensor(f"OUTP{L}", [128, 4 * T], BF16))
            WQS = sco.enter_context(nc.sbuf_tensor(f"WQS{L}", [128, 2 * DT * 128], BF16))
            WKS = sco.enter_context(nc.sbuf_tensor(f"WKS{L}", [128, 2 * DT * 128], BF16))
            QP = sco.enter_context(nc.sbuf_tensor(f"QP{L}", [128, T], BF16))
            KP = sco.enter_context(nc.sbuf_tensor(f"KP{L}", [128, T], BF16))
            SH = sco.enter_context(nc.sbuf_tensor(f"SH{L}", [128, T], BF16))
            PT = sco.enter_context(nc.sbuf_tensor(f"PT{L}", [128, 5 * 512], BF16))
            ZR = sco.enter_context(nc.sbuf_tensor(f"ZR{L}", [1, 1024], F32))
            ZB = sco.enter_context(nc.sbuf_tensor(f"ZB{L}", [64, 512], F32))
            OSTG = sco.enter_context(nc.sbuf_tensor(f"OSTG{L}", [64, 512], BF16))
            WVB = sco.enter_context(nc.sbuf_tensor(f"WVB{L}", [128, DT * 512], BF16))
            WOB = sco.enter_context(nc.sbuf_tensor(f"WOB{L}", [128, 8 * 128], BF16))

            def gtab(g):
                c.wait_buf(g, "ROPETAB")
                c.wait_buf(g, "SBUF")
                c.dma(g, "qTAB", COS[:], dram["rope_cos"])
                c.dma(g, "qTAB", SIN[:], dram["rope_sin"])
            block.gpsimd(gtab)
            tab_ready = c.cnt["qTAB"]

            chunk_ct = 0
            for hh in range(2):
                psA = contextlib.ExitStack()
                ps_v = psA.enter_context(nc.psum_tensor(f"psv{L}{hh}", [128, 1024], F32))

                def vones(v):
                    c.wait_buf(v, "VT")
                    c.wait_buf(v, "SBUF")
                    for tt in range(NKT):
                        a3 = VT[:, tt * 520:(tt + 1) * 520].rearrange(
                            "p (h e) -> p h e", h=8)
                        i = v.memset(a3[:, :, 64:65], 1.0)
                    c.inc(i, "vC")
                block.vector(vones)
                ones_done = c.cnt["vC"]

                def gldv(g, hh=hh):
                    c.wait_buf(g, "WVB")
                    if hh == 0:
                        c.wait_buf(g, "SBUF")
                    for k in range(DT):
                        c.dma(g, "qWV", WVB[:, k * 512:(k + 1) * 512],
                              wvT[k * 128:(k + 1) * 128, hh * 512:(hh + 1) * 512])
                block.gpsimd(gldv)
                wv_ready = c.cnt["qWV"]

                for tt in range(NKT):
                    vsl = tt % 2
                    def tf(t_, tt=tt, vsl=vsl, hh=hh, wv_ready=wv_ready):
                        c.wait(t_, "qWV", wv_ready)
                        if tt == 0:
                            c.wait(t_, "vC", ln_done)
                            c.wait_buf(t_, "PSUM")
                        else:
                            c.wait_buf(t_, f"ps_v{vsl}")
                        for k in range(DT):
                            i = t_.matmul(ps_v[:, vsl * 512:(vsl + 1) * 512],
                                          ap_lnx(k, tt * 128, (tt + 1) * 128),
                                          WVB[:, k * 512:(k + 1) * 512],
                                          start=(k == 0), stop=(k == DT - 1))
                        c.inc(i, "tC")
                    block.tensor(tf)
                    mm_done = c.cnt["tC"]

                    sfv = c.reserve("sC")
                    def sf(sc, tt=tt, vsl=vsl, mm_done=mm_done, ones_done=ones_done,
                           ps_v=ps_v):
                        c.wait(sc, "tC", mm_done)
                        c.wait(sc, "vC", ones_done)
                        dst = VT[:, tt * 520:(tt + 1) * 520].rearrange(
                            "p (h e) -> p h e", h=8)[:, :, 0:64]
                        src = ps_v[:, vsl * 512:(vsl + 1) * 512].rearrange(
                            "p (h e) -> p h e", h=8)
                        i = sc.copy(dst, src)
                        c.inc_to(i, "sC")
                    c.sc(sf)
                    c.use(f"ps_v{vsl}", "sC", sfv)
                c.use("WVB", "tC", c.cnt["tC"])
                v_done = c.cnt["sC"]
                psA.close()

                psB = contextlib.ExitStack()
                ps_s = psB.enter_context(nc.psum_tensor(f"pss{L}{hh}", [128, 5 * 512], F32))
                ps_o = psB.enter_context(nc.psum_tensor(f"pso{L}{hh}", [65, 2 * 512], F32))
                ps_zb = psB.enter_context(nc.psum_tensor(f"pszb{L}{hh}", [64, 512], F32))
                ps_qk = ps_s

                for pp2 in range(4):
                    pair = 4 * hh + pp2
                    buf = pp2 % 2

                    if pp2 == 0:
                        def gldqk(g, pair=pair, buf=buf):
                            c.wait_buf(g, f"WQS{buf}")
                            c.wait(g, f"qWQK{buf}", c.cnt[f"qWQK{buf}"])
                            if pair % 4 == 0:
                                c.wait_buf(g, "SBUF")
                            for k in range(DT):
                                c.dma(g, f"qWQK{buf}", WQS[:, (buf * DT + k) * 128:(buf * DT + k + 1) * 128],
                                      wqT[k * 128:(k + 1) * 128, pair * 128:(pair + 1) * 128])
                                c.dma(g, f"qWQK{buf}", WKS[:, (buf * DT + k) * 128:(buf * DT + k + 1) * 128],
                                      wkT[k * 128:(k + 1) * 128, pair * 128:(pair + 1) * 128])
                        block.gpsimd(gldqk)
                    wqk_ready = (f"qWQK{buf}", c.cnt[f"qWQK{buf}"])

                    for which in ("q", "k"):
                        wsl = WQS if which == "q" else WKS
                        PBUF = QP if which == "q" else KP
                        for rr in range(2):
                            def tproj(t_, which=which, wsl=wsl, rr=rr, buf=buf, pair=pair,
                                      wqk_ready=wqk_ready, v_done=v_done):
                                c.wait(t_, wqk_ready[0], wqk_ready[1])
                                c.wait(t_, "qTAB", tab_ready)
                                if which == "q" and rr == 0 and pair % 4 == 0:
                                    c.wait(t_, "vC", ln_done)
                                    c.wait(t_, "sC", v_done)
                                    c.wait_buf(t_, "PSUM")
                                c.wait_buf(t_, "ps_qk")
                                c.wait_buf(t_, "ps_s0")
                                c.wait_buf(t_, "ps_s1")
                                for qx in range(2):
                                    q = 2 * rr + qx
                                    for k in range(DT):
                                        i = t_.matmul(
                                            ps_qk[:, qx * 512:(qx + 1) * 512],
                                            wsl[:, (buf * DT + k) * 128:(buf * DT + k + 1) * 128],
                                            ap_lnx(k, q * 512, (q + 1) * 512),
                                            start=(k == 0), stop=(k == DT - 1))
                                c.inc(i, "tC")
                            block.tensor(tproj)
                            proj_done = c.cnt["tC"]

                            ev_done = c.reserve("sC")
                            pbw = c.snap("QP" if which == "q" else "KP")
                            def sev(sc, PBUF=PBUF, rr=rr, proj_done=proj_done, pbw=pbw, ps_qk=ps_qk):
                                c.wait(sc, "tC", proj_done)
                                c.wait_list(sc, pbw)
                                i = sc.copy(PBUF[:, rr * 1024:(rr + 1) * 1024], ps_qk[:, 0:1024])
                                c.inc_to(i, "sC")
                            c.sc(sev)
                            c.use("ps_qk", "sC", ev_done)

                        if which == "q":
                            def gsh(g, PBUF=PBUF, ev_done=ev_done):
                                c.wait(g, "sC", ev_done)
                                c.wait_buf(g, "SH")
                                for (d0, s0) in ((0, 32), (32, 0), (64, 96), (96, 64)):
                                    c.dma(g, "qSH", SH[d0:d0 + 32, :], PBUF[s0:s0 + 32, :])
                            block.gpsimd(gsh)
                            sh_done = c.cnt["qSH"]

                            def vr(v, PBUF=PBUF, sh_done=sh_done):
                                c.wait(v, "qSH", sh_done)
                                v.tensor_tensor(PBUF[:, :], PBUF[:, :], COS[:, :], ALU.mult)
                                i = v.tensor_tensor(SH[:, :], SH[:, :], SIN[:, :], ALU.mult)
                                c.fence(v, "vC", i)
                                i = v.tensor_tensor(PBUF[:, :], PBUF[:, :], SH[:, :], ALU.add)
                                c.inc(i, "vC")
                            block.vector(vr)
                            c.use("SH", "vC", c.cnt["vC"])
                        else:
                            rope_hd = []
                            prev_shk = None
                            for hf in range(2):
                                p0 = 64 * hf
                                def gshk(g, hf=hf, p0=p0, ev_done=ev_done,
                                         prev_shk=prev_shk):
                                    if hf == 0:
                                        c.wait(g, "sC", ev_done)
                                        c.wait_buf(g, "SH")
                                    else:
                                        c.wait(g, "qSH", prev_shk)
                                    for (d0, s0) in ((0, 32), (32, 0)):
                                        c.dma(g, "qSH", SH[p0 + d0:p0 + d0 + 32, :],
                                              KP[p0 + s0:p0 + s0 + 32, :])
                                block.gpsimd(gshk)
                                shk = c.cnt["qSH"]
                                prev_shk = shk

                                def vrk(v, p0=p0, shk=shk):
                                    c.wait(v, "qSH", shk)
                                    v.tensor_tensor(KP[p0:p0 + 64, :], KP[p0:p0 + 64, :],
                                                    COS[p0:p0 + 64, :], ALU.mult)
                                    i = v.tensor_tensor(SH[p0:p0 + 64, :],
                                                        SH[p0:p0 + 64, :],
                                                        SIN[p0:p0 + 64, :], ALU.mult)
                                    c.fence(v, "vC", i)
                                    i = v.tensor_tensor(KP[p0:p0 + 64, :],
                                                        KP[p0:p0 + 64, :],
                                                        SH[p0:p0 + 64, :], ALU.add)
                                    c.inc(i, "vC")
                                block.vector(vrk)
                                rope_hd.append(c.cnt["vC"])
                            c.use("SH", "vC", c.cnt["vC"])
                    rope0_done, rope1_done = rope_hd
                    c.use(f"WQS{buf}", "tC", c.cnt["tC"])

                    if pp2 < 3:
                        def gldqk2(g, pair=pair + 1, buf=buf ^ 1):
                            c.wait_buf(g, f"WQS{buf}")
                            c.wait(g, f"qWQK{buf}", c.cnt[f"qWQK{buf}"])
                            for k in range(DT):
                                c.dma(g, f"qWQK{buf}", WQS[:, (buf * DT + k) * 128:(buf * DT + k + 1) * 128],
                                      wqT[k * 128:(k + 1) * 128, pair * 128:(pair + 1) * 128])
                                c.dma(g, f"qWQK{buf}", WKS[:, (buf * DT + k) * 128:(buf * DT + k + 1) * 128],
                                      wkT[k * 128:(k + 1) * 128, pair * 128:(pair + 1) * 128])
                        block.gpsimd(gldqk2)

                    jobs = []
                    for sub in range(2):
                        for qc in range(NQC):
                            nkt = 4 * qc + 4
                            for kt in range(nkt):
                                jobs.append((sub, qc, kt, nkt))
                    NJ = len(jobs)
                    G, DDEL = 5, 1
                    meta = {}
                    sched = []

                    def emit_ts(j, rope0_done=rope0_done, rope1_done=rope1_done):
                        sub, qc, kt, nkt = jobs[j]
                        slot = j % 5
                        pb = 64 * sub
                        def ts(t_, j=j, qc=qc, kt=kt, slot=slot, pb=pb,
                               rope0_done=rope0_done, rope1_done=rope1_done):
                            if j == 0:
                                c.wait(t_, "vC", rope0_done)
                            elif j == 40:
                                c.wait(t_, "vC", rope1_done)
                            c.wait_buf(t_, f"ps_s{slot}")
                            i = t_.matmul(ps_s[:, slot * 512:(slot + 1) * 512],
                                          KP[pb:pb + 64, kt * 128:(kt + 1) * 128],
                                          QP[pb:pb + 64, qc * 512:(qc + 1) * 512],
                                          start=True, stop=True)
                            c.inc(i, "tC")
                        block.tensor(ts)
                        s_done = c.cnt["tC"]

                        e_done = c.reserve("sC")
                        ptw = c.snap(f"PT{slot}")
                        def sexp(sc, slot=slot, s_done=s_done, ptw=ptw, ps_s=ps_s):
                            c.wait(sc, "tC", s_done)
                            c.wait_list(sc, ptw)
                            i = sc.activation(PT[:, slot * 512:(slot + 1) * 512],
                                              ps_s[:, slot * 512:(slot + 1) * 512],
                                              AF.Exp, scale=0.125)
                            c.inc_to(i, "sC")
                        c.sc(sexp)
                        c.use(f"ps_s{slot}", "sC", e_done)

                        if kt >= 4 * qc:
                            delta = 512 * qc - 128 * kt
                            def gmask(g, slot=slot, delta=delta, e_done=e_done):
                                c.wait(g, "sC", e_done)
                                i = g.affine_select(
                                    PT[:, slot * 512:(slot + 1) * 512],
                                    PT[:, slot * 512:(slot + 1) * 512],
                                    [[1, 512]], ALU.is_ge, 0.0,
                                    base=delta, channel_multiplier=-1)
                                c.inc(i, "gC")
                            block.gpsimd(gmask)
                            mg = ("gC", c.cnt["gC"])
                            c.add_use(f"PT{slot}", "gC", c.cnt["gC"])
                        else:
                            mg = ("sC", e_done)
                        meta[j] = (slot, mg)

                    def emit_tpv(j, pos, pp2=pp2):
                        nonlocal chunk_ct
                        sub, qc, kt, nkt = jobs[j]
                        slot, mg = meta[j]
                        hl = 2 * pp2 + sub
                        oslot = chunk_ct % 2
                        if kt == 0:
                            for p in [p for p in sched if p[1][2] == oslot]:
                                sched.remove(p)
                                emit_tail(p[1])
                        def tpv(t_, kt=kt, hl=hl, slot=slot, nkt=nkt, mg=mg, oslot=oslot):
                            c.wait(t_, mg[0], mg[1])
                            if kt == 0:
                                c.wait_buf(t_, f"ps_o{oslot}")
                            i = t_.matmul(
                                ps_o[:, oslot * 512:(oslot + 1) * 512],
                                VT[:, kt * 520 + hl * 65: kt * 520 + (hl + 1) * 65],
                                PT[:, slot * 512:(slot + 1) * 512],
                                start=(kt == 0), stop=(kt == nkt - 1))
                            c.inc(i, "tC")
                        block.tensor(tpv)
                        c.use(f"PT{slot}", "tC", c.cnt["tC"])
                        if kt == nkt - 1:
                            pv_done = c.cnt["tC"]
                            def vz(v, pv_done=pv_done, oslot=oslot):
                                c.wait(v, "tC", pv_done)
                                c.wait_buf(v, f"ZR{oslot}")
                                i = v.reciprocal(ZR[:, oslot * 512:(oslot + 1) * 512],
                                                 ps_o[64:65, oslot * 512:(oslot + 1) * 512])
                                c.inc(i, "vC")
                            block.vector(vz)
                            zr_done = c.cnt["vC"]
                            sched.append((pos + DDEL, (sub, qc, oslot, zr_done)))
                            chunk_ct += 1

                    def emit_tail(payload, pp2=pp2):
                        sub, qc, oslot, zr_done = payload
                        def tzb(t_, zr_done=zr_done, oslot=oslot):
                            c.wait(t_, "vC", zr_done)
                            c.wait_buf(t_, "ps_zb")
                            i = t_.matmul(ps_zb[:, :], ONE1F[:, :],
                                          ZR[:, oslot * 512:(oslot + 1) * 512],
                                          start=True, stop=True)
                            c.inc(i, "tC")
                        block.tensor(tzb)
                        zb_done = c.cnt["tC"]
                        c.use(f"ZR{oslot}", "tC", zb_done)

                        def vev(v, qc=qc, pp2=pp2, sub=sub, oslot=oslot, zb_done=zb_done):
                            c.wait(v, "tC", zb_done)
                            if sub == 1:
                                c.wait_buf(v, "OSTG")
                            i = v.tensor_copy(ZB[:, :], ps_zb[:, :])
                            c.fence(v, "vC", i)
                            if sub == 0:
                                dst = OUTP[0:64, pp2 * T + qc * 512: pp2 * T + (qc + 1) * 512]
                            else:
                                dst = OSTG[:, :]
                            i = v.tensor_tensor(dst, ps_o[0:64, oslot * 512:(oslot + 1) * 512],
                                                ZB[:, :], ALU.mult)
                            c.inc(i, "vC")
                        block.vector(vev)
                        c.use("ps_zb", "vC", c.cnt["vC"])
                        c.use(f"ps_o{oslot}", "vC", c.cnt["vC"])
                        c.use("ZB", "vC", c.cnt["vC"])
                        oev = c.cnt["vC"]

                        if sub == 1:
                            def gmv(g, qc=qc, pp2=pp2, oev=oev):
                                c.wait(g, "vC", oev)
                                c.dma(g, "qMV",
                                      OUTP[64:128, pp2 * T + qc * 512: pp2 * T + (qc + 1) * 512],
                                      OSTG[:, :])
                            block.gpsimd(gmv)
                            c.use("OSTG", "qMV", c.cnt["qMV"])

                    NG = (NJ + G - 1) // G
                    for gp in range(NG + 1):
                        for p in [p for p in sched if p[0] <= gp]:
                            sched.remove(p)
                            emit_tail(p[1])
                        if gp >= 1:
                            for j in range(G * (gp - 1), min(G * gp, NJ)):
                                emit_tpv(j, gp)
                        if gp < NG:
                            for j in range(G * gp, min(G * (gp + 1), NJ)):
                                emit_ts(j)
                    for p in sorted(sched, key=lambda t: t[0]):
                        emit_tail(p[1])
                    del sched[:]

                    c.use("QP", "tC", c.cnt["tC"])
                    c.use("KP", "tC", c.cnt["tC"])
                attn_v = c.cnt["vC"]
                attn_g = c.cnt["qMV"]
                psB.close()
                c.use("PSUM", "vC", attn_v)
                c.add_use("PSUM", "sC", c.cnt["sC"])

                psC = contextlib.ExitStack()
                ps_r = psC.enter_context(nc.psum_tensor(f"psr{L}{hh}", [128, 1024], F32))
                for mt in range(DT):
                    wsl = mt % 2
                    def gldo2(g, mt=mt, wsl=wsl, hh=hh):
                        c.wait_buf(g, f"WOB{wsl}")
                        for k in range(4):
                            c.dma(g, f"qWO{wsl}", WOB[:, (wsl * 4 + k) * 128:(wsl * 4 + k + 1) * 128],
                                  woT[hh * 512 + k * 128: hh * 512 + (k + 1) * 128,
                                      mt * 128:(mt + 1) * 128])
                    block.gpsimd(gldo2)
                    wo_ready = c.cnt[f"qWO{wsl}"]
                    for q in range(NQC):
                        rsl = (mt * NQC + q) % 2
                        def tro(t_, mt=mt, q=q, wsl=wsl, rsl=rsl, wo_ready=wo_ready):
                            c.wait(t_, f"qWO{wsl}", wo_ready)
                            if mt == 0 and q == 0:
                                c.wait(t_, "vC", attn_v)
                                c.wait(t_, "qMV", attn_g)
                                c.wait_buf(t_, "PSUM")
                            c.wait_buf(t_, f"ps_r{rsl}")
                            for k in range(4):
                                i = t_.matmul(ps_r[:, rsl * 512:(rsl + 1) * 512],
                                              WOB[:, (wsl * 4 + k) * 128:(wsl * 4 + k + 1) * 128],
                                              OUTP[:, k * T + q * 512: k * T + (q + 1) * 512],
                                              start=(k == 0), stop=(k == 3))
                            c.inc(i, "tC")
                        block.tensor(tro)
                        o_done = c.cnt["tC"]

                        def vadd(v, mt=mt, q=q, rsl=rsl, o_done=o_done):
                            c.wait(v, "tC", o_done)
                            i = v.tensor_tensor(ap_x(mt, q * 512, (q + 1) * 512),
                                                ap_x(mt, q * 512, (q + 1) * 512),
                                                ps_r[:, rsl * 512:(rsl + 1) * 512],
                                                ALU.add)
                            c.inc(i, "vC")
                        block.vector(vadd)
                        c.use(f"ps_r{rsl}", "vC", c.cnt["vC"])
                    c.use(f"WOB{wsl}", "tC", c.cnt["tC"])
                c.use("VT", "sC", c.cnt["sC"])
                c.add_use("VT", "tC", c.cnt["tC"])
                c.use("OUTP", "tC", c.cnt["tC"])
                psC.close()
                c.use("PSUM", "vC", c.cnt["vC"])
            resid = c.cnt["vC"]
            c.use("LNX", "tC", c.cnt["tC"])
            c.use("PSUM", "vC", resid)
            c.use("ROPETAB", "vC", c.cnt["vC"])
            sco.close()
            c.sbuf_epoch(extra=("qMV", "qSH"))
            return resid


        def ffn0():
            C1 = float(np.log(T / COOR_K))
            ECAP = float(np.exp(CAP))

            scA = contextlib.ExitStack()
            LNXF = scA.enter_context(nc.sbuf_tensor("LNXF", [128, DT * T], BF16))

            def ap_lnx(t, c0=0, c1=T):
                return LNXF[:, t * T + c0: t * T + c1]

            ln_done = layernorm("ln2", LNXF)

            ss = contextlib.ExitStack()
            SRO = ss.enter_context(nc.sbuf_tensor("SRO", [1, T], F32))
            psU = ss.enter_context(nc.psum_tensor("psU", [1, T], F32))
            psS = ss.enter_context(nc.psum_tensor("psS", [1, 1], F32))
            psT = ss.enter_context(nc.psum_tensor("psT", [128, 16], F32))

            def t_s(t_):
                c.wait(t_, "vC", ln_done)
                c.wait_buf(t_, "PSUM")
                for t in range(DT):
                    for q in range(NQC):
                        t_.matmul(psU[:, q * 512:(q + 1) * 512],
                                  SMALL[:, t:t + 1], ap_x(t, q * 512, (q + 1) * 512),
                                  start=(t == 0), stop=(t == DT - 1))
                for t in range(DT):
                    i = t_.matmul(psS[:, :], SMALL[:, t:t + 1], SV[:, 1:2],
                                  start=(t == 0), stop=(t == DT - 1))
                c.inc(i, "tC")
            block.tensor(t_s)
            u_done = c.cnt["tC"]

            SRO2 = ss.enter_context(nc.sbuf_tensor("SRO2", [1, T], F32))

            def v_s(v, u_done=u_done):
                c.wait(v, "tC", u_done)
                i = v.tensor_copy(SRO2[:, :], psU[:, :])
                c.fence(v, "vC", i)
                i = v.tensor_copy(SV[0:1, 3:4], psS[:, :])
                c.fence(v, "vC", i)
                i = v.tensor_tensor(SRO[:, :], psU[:, :], IREP[0:1, :], ALU.mult)
                c.fence(v, "vC", i)
                i = v.scalar_tensor_tensor(SRO[:, :], MREP[0:1, :], SV[0:1, 3:4],
                                           SRO[:, :], ALU.mult, ALU.subtract)
                c.fence(v, "vC", i)
                i = v.tensor_scalar(SRO[:, :], SRO[:, :], -1.0, None, ALU.mult)
                c.inc(i, "vC")
            block.vector(v_s)
            sro_done = c.cnt["vC"]

            def t_tr(t_, sro_done=sro_done):
                c.wait(t_, "vC", sro_done)
                for cc in range(16):
                    i = t_.transpose(psT[:, cc:cc + 1], SRO[:, cc * 128:(cc + 1) * 128],
                                     IDN[0:1, 0:1])
                c.inc(i, "tC")
            block.tensor(t_tr)
            str_done = c.cnt["tC"]

            def v_sig(v, str_done=str_done):
                c.wait(v, "tC", str_done)
                i = v.tensor_scalar(SIG[:, :], psT[:, :], -C1, CAP, ALU.add, ALU.min)
                c.inc(i, "vC")
            block.vector(v_sig)
            sg = c.cnt["vC"]

            sig_ready = c.reserve("sC")
            def s_sig(sc, sg=sg):
                c.wait(sc, "vC", sg)
                i = sc.activation(SIG[:, :], SIG[:, :], AF.Exp)
                c.inc_to(i, "sC")
            c.sc(s_sig)
            c.use("PSUM", "sC", sig_ready)
            if debug_stage == "sig":
                print("EMITTING STDBG")
                def stdbg(s):
                    c.wait(s, "sC", sig_ready)
                    c.dma(s, "qX", out[1:2, :], SRO[:, :])
                    c.dma(s, "qX", out[2:130, 0:16], SIG[:, :])
                    c.dma(s, "qX", out[130:131, :], SRO2[:, :])
                    c.dma(s, "qX", out[515:516, 48:56], IREP[0:1, 0:8])
                    c.dma(s, "qX", out[516:517, 48:56], MREP[0:1, 0:8])
                    c.wait(s, "qX", c.cnt["qX"])
                c.sc_flush()
                block.sync(stdbg)
                return ("SIGSTOP", sig_ready)
            ss.close()

            rs = contextlib.ExitStack()
            psR = rs.enter_context(nc.psum_tensor("psR", [128, T], F32))
            for t in range(DT):
                sq_done = c.reserve("sC")
                scb_w = c.snap("SCB")
                def s_sq(sc, t=t, scb_w=scb_w):
                    c.wait_list(sc, scb_w)
                    i = sc.activation(SCB[:, :], ap_lnx(t), AF.Square)
                    c.inc_to(i, "sC")
                c.sc(s_sq)

                def t_sq(t_, t=t, sq_done=sq_done):
                    c.wait(t_, "sC", sq_done)
                    if t == 0:
                        c.wait_buf(t_, "PSUM")
                    for q in range(NQC):
                        i = t_.matmul(psR[:, q * 512:(q + 1) * 512], ONES[:],
                                      SCB[:, q * 512:(q + 1) * 512],
                                      start=(t == 0), stop=(t == DT - 1))
                    c.inc(i, "tC")
                block.tensor(t_sq)
                c.use("SCB", "tC", c.cnt["tC"])
            ssq_done = c.cnt["tC"]

            f1 = c.reserve("sC")
            rr1 = c.reserve("sC")
            def s_rr(sc, ssq_done=ssq_done, f1=f1):
                c.wait(sc, "tC", ssq_done)
                i = sc.activation(MREP[:, :], psR[:, :], AF.Ln)
                c.fence_to(sc, "sC", i, f1)
                i = sc.activation(MREP[:, :], MREP[:, :], AF.Exp, scale=0.5)
                c.inc_to(i, "sC")
            c.sc(s_rr)
            c.use("PSUM", "sC", rr1)
            rs.close()

            def v_rr(v, rr1=rr1):
                c.wait(v, "sC", rr1)
                i = v.tensor_scalar(MREP[:, :], MREP[:, :], 1e-12, None, ALU.max)
                c.fence(v, "vC", i)
                i = v.reciprocal(MREP[:, :], MREP[:, :])
                c.fence(v, "vC", i)
                v.tensor_scalar(SV[:, 4:12], SMALL[:, 8:16], 32.0, None, ALU.mult)
                i = v.tensor_scalar(SV[:, 12:20], SMALL[:, 16:24], 32.0, None, ALU.mult)
                c.fence(v, "vC", i)
            block.vector(v_rr)

            scB = contextlib.ExitStack()
            HB = scB.enter_context(nc.sbuf_tensor("HB0", [128, DT * T], BF16))
            for t in range(DT):
                def v_hh(v, t=t):
                    if t == 0:
                        c.wait_buf(v, "SBUF")
                    i = v.tensor_tensor(HB[:, t * T:(t + 1) * T], ap_lnx(t),
                                        MREP[:, :], ALU.mult)
                    c.fence(v, "vC", i)
                    i = v.tensor_scalar(HB[:, t * T:(t + 1) * T],
                                        HB[:, t * T:(t + 1) * T],
                                        SV[:, 12 + t:13 + t], None, ALU.mult)
                    c.fence(v, "vC", i)
                block.vector(v_hh)
            for t in range(DT):
                def v_hl(v, t=t):
                    i = v.tensor_tensor(ap_lnx(t), ap_lnx(t), MREP[:, :], ALU.mult)
                    c.fence(v, "vC", i)
                    i = v.tensor_scalar(ap_lnx(t), ap_lnx(t),
                                        SV[:, 4 + t:5 + t], None, ALU.mult)
                    c.fence(v, "vC", i)
                block.vector(v_hl)
            h_done = c.cnt["vC"]

            ls = contextlib.ExitStack()
            HL = ls.enter_context(nc.sbuf_tensor("HL0", [128, 4 * T], BF16))
            WL = ls.enter_context(nc.sbuf_tensor("WL0", [128, DT * 512], BF16))
            WB = ls.enter_context(nc.sbuf_tensor("WB0", [128, 4 * 1024], BF16))
            psL = ls.enter_context(nc.psum_tensor("psL", [128, 2 * 512], F32))

            def g_wl(g):
                c.wait_buf(g, "SBUF")
                for k in range(DT):
                    c.dma(g, "qF0", WL[:, k * 512:(k + 1) * 512],
                          dram["w1lT"][k * 128:(k + 1) * 128, :])
                for k in range(4):
                    c.dma(g, "qF1", WB[:, k * 1024:(k + 1) * 1024],
                          dram["w2lT"][k * 128:(k + 1) * 128, :])
            block.gpsimd(g_wl)
            wl_ready = c.cnt["qF0"]
            wb_ready = c.cnt["qF1"]

            for m in range(4):
                for ch in range(NQC):
                    pp = (m * NQC + ch) % 2
                    def t_l1(t_, m=m, ch=ch, pp=pp):
                        if m == 0 and ch == 0:
                            c.wait(t_, "qF0", wl_ready)
                            c.wait(t_, "vC", h_done)
                            c.wait(t_, "sC", rr1)
                        c.wait_buf(t_, f"psL{pp}")
                        for k in range(DT):
                            i = t_.matmul(psL[:, pp * 512:(pp + 1) * 512],
                                          WL[:, k * 512 + m * 128: k * 512 + (m + 1) * 128],
                                          ap_lnx(k, ch * 512, (ch + 1) * 512),
                                          start=(k == 0), stop=(k == DT - 1))
                        c.inc(i, "tC")
                    block.tensor(t_l1)
                    mm_done = c.cnt["tC"]

                    l1v = c.reserve("sC")
                    def s_l1(sc, m=m, ch=ch, pp=pp, mm_done=mm_done):
                        c.wait(sc, "tC", mm_done)
                        i = sc.activation(HL[:, m * T + ch * 512: m * T + (ch + 1) * 512],
                                          psL[:, pp * 512:(pp + 1) * 512],
                                          AF.Gelu, bias=SMALL[:, 40 + m:41 + m])
                        c.inc_to(i, "sC")
                    c.sc(s_l1)
                    c.use(f"psL{pp}", "sC", l1v)
            l1_done = c.cnt["sC"]

            for mt in range(DT):
                for ch in range(NQC):
                    pp = (mt * NQC + ch) % 2
                    def t_l2(t_, mt=mt, ch=ch, pp=pp):
                        if mt == 0 and ch == 0:
                            c.wait(t_, "qF1", wb_ready)
                            c.wait(t_, "sC", l1_done)
                        c.wait_buf(t_, f"psL{pp}")
                        for k in range(4):
                            i = t_.matmul(psL[:, pp * 512:(pp + 1) * 512],
                                          WB[:, k * 1024 + mt * 128: k * 1024 + (mt + 1) * 128],
                                          HL[:, k * T + ch * 512: k * T + (ch + 1) * 512],
                                          start=(k == 0), stop=(k == 3))
                        c.inc(i, "tC")
                    block.tensor(t_l2)
                    mm2 = c.cnt["tC"]

                    def v_l2(v, mt=mt, ch=ch, pp=pp, mm2=mm2):
                        c.wait(v, "tC", mm2)
                        i = v.scalar_tensor_tensor(
                            ap_x(mt, ch * 512, (ch + 1) * 512),
                            psL[:, pp * 512:(pp + 1) * 512],
                            SMALL[:, 24 + mt:25 + mt],
                            ap_x(mt, ch * 512, (ch + 1) * 512),
                            ALU.add, ALU.add)
                        c.fence(v, "vC", i)
                    block.vector(v_l2)
                    c.use(f"psL{pp}", "vC", c.cnt["vC"])
            light_done = c.cnt["vC"]
            c.use("LNX", "tC", c.cnt["tC"])
            ls.close()
            c.sbuf_epoch()

            hs = contextlib.ExitStack()
            WH1 = hs.enter_context(nc.sbuf_tensor("WH1", [128, 2 * DT * 128], BF16))
            HST = hs.enter_context(nc.sbuf_tensor("HST", [128, 2 * T], BF16))
            psH = hs.enter_context(nc.psum_tensor("psH", [128, 2 * 1024], F32))
            psZ = hs.enter_context(nc.psum_tensor("psZ", [128, 1], F32))

            citer = [0]

            def emit_coor_iter():
                n = citer[0]

                def v_c1(v, n=n):
                    if n == 0:
                        c.wait(v, "sC", sig_ready)
                    c.wait_buf(v, "SIGt")
                    i = v.tensor_scalar(SV[:, 21:37], SIG[:, :], 1.0, None, ALU.min)
                    c.fence(v, "vC", i)
                    i = v.reduce_sum(SV[:, 20:21], SV[:, 21:37], mybir.AxisListType.X)
                    c.inc(i, "vC")
                block.vector(v_c1)
                part = c.cnt["vC"]

                def t_c(t_, part=part, n=n):
                    c.wait(t_, "vC", part)
                    c.wait_buf(t_, "psZ")
                    if n == 0:
                        c.wait(t_, "vC", light_done)
                    i = t_.matmul(psZ[:, :], ONESF[:, :], SV[:, 20:21],
                                  start=True, stop=True)
                    c.inc(i, "tC")
                block.tensor(t_c)
                zd = c.cnt["tC"]

                def v_c2(v, zd=zd):
                    c.wait(v, "tC", zd)
                    i = v.reciprocal(SV[:, 2:3], psZ[:, :])
                    c.fence(v, "vC", i)
                    i = v.tensor_scalar(SIG[:, :], SIG[:, :], SV[:, 2:3],
                                        float(COOR_K), ALU.mult, ALU.mult)
                    c.fence(v, "vC", i)
                    i = v.tensor_scalar(SIG[:, :], SIG[:, :], ECAP, None, ALU.min)
                    c.inc(i, "vC")
                block.vector(v_c2)
                c.use("psZ", "vC", c.cnt["vC"])
                c.use("SIGt", "vC", c.cnt["vC"])
                citer[0] += 1

            for m in range(32):
                buf = m % 2
                qw = "qF0" if buf == 0 else "qF1"

                def g_wh1(g, m=m, buf=buf, qw=qw):
                    c.wait_buf(g, f"WH1{buf}")
                    if m < 2:
                        c.wait_buf(g, "SBUF")
                    c.dma(g, qw,
                          WH1[:, buf * DT * 128:(buf + 1) * DT * 128].rearrange(
                              "p (k cc) -> p k cc", cc=128),
                          dram["w1hT_blk"][m * 1024:(m + 1) * 1024, :].rearrange(
                              "(k p) cc -> p k cc", p=128))
                block.gpsimd(g_wh1)
                wh_ready = c.cnt[qw]

                for half in range(2):
                    def t_h1(t_, m=m, half=half, buf=buf, qw=qw, wh_ready=wh_ready):
                        c.wait(t_, qw, wh_ready)
                        if m == 0 and half == 0:
                            c.wait(t_, "vC", light_done)
                        c.wait_buf(t_, f"psHh{half}")
                        for ch in range(2):
                            for k in range(DT):
                                i = t_.matmul(
                                    psH[:, half * 1024 + ch * 512: half * 1024 + (ch + 1) * 512],
                                    WH1[:, (buf * DT + k) * 128:(buf * DT + k + 1) * 128],
                                    HB[:, k * T + half * 1024 + ch * 512: k * T + half * 1024 + (ch + 1) * 512],
                                    start=(k == 0), stop=(k == DT - 1))
                        c.inc(i, "tC")
                    block.tensor(t_h1)
                    h1_done = c.cnt["tC"]

                    h1v = c.reserve("sC")
                    hst_w = c.snap(f"HSTb{m % 2}")
                    def s_h1(sc, m=m, half=half, h1_done=h1_done, hst_w=hst_w):
                        c.wait(sc, "tC", h1_done)
                        c.wait_list(sc, hst_w)
                        i = sc.activation(
                            HST[:, (m % 2) * T + half * 1024: (m % 2) * T + (half + 1) * 1024],
                            psH[:, half * 1024:(half + 1) * 1024],
                            AF.Gelu, bias=SMALL[:, 44 + m:45 + m])
                        c.inc_to(i, "sC")
                    c.sc(s_h1)
                    c.use(f"psHh{half}", "sC", h1v)
                hst_done = c.cnt["sC"]

                qh = "qF2" if m % 2 == 0 else "qF2b"

                def g_hst(g, m=m, hst_done=hst_done, qh=qh):
                    c.wait(g, "sC", hst_done)
                    c.dma(g, qh, Hdram[m * 128:(m + 1) * 128, :],
                          HST[:, (m % 2) * T:((m % 2) + 1) * T])
                block.gpsimd(g_hst)
                c.use(f"HSTb{m % 2}", qh, c.cnt[qh])

                while citer[0] < min(49, (m + 1) * 2):
                    emit_coor_iter()
            while citer[0] < 49:
                emit_coor_iter()

            def v_fs(v):
                c.wait_buf(v, "SIGt")
                i = v.tensor_scalar(SIG[:, :], SIG[:, :], 1.0, None, ALU.min)
                c.inc(i, "vC")
            block.vector(v_fs)
            sig_final = c.cnt["vC"]
            heavy1_done_t = c.cnt["tC"]
            heavy1_done_q = (c.cnt["qF2"], c.cnt["qF2b"])
            c.use("PSUM", "vC", sig_final)
            hs.close()
            scB.close()
            scA.close()
            c.sbuf_epoch(extra=("qF2", "qF2b"))

            h2 = contextlib.ExitStack()
            SFL = h2.enter_context(nc.sbuf_tensor("SFL", [1, T], F32))
            psB = h2.enter_context(nc.psum_tensor("psB", [128, T], F32))
            psC2 = h2.enter_context(nc.psum_tensor("psC2", [128, 16], F32))
            psP2 = h2.enter_context(nc.psum_tensor("psP2", [128, 16], F32))

            SIGB = IREP
            MASKB = MREP

            def t_sf(t_, sig_final=sig_final):
                c.wait(t_, "vC", sig_final)
                c.wait(t_, "sC", c.cnt["sC"])
                c.wait_buf(t_, "SBUF")
                for cc in range(16):
                    i = t_.transpose(psB[0:1, cc * 128:(cc + 1) * 128],
                                     SIG[:, cc:cc + 1], IDN[:, :])
                c.inc(i, "tC")
            block.tensor(t_sf)
            sf_done = c.cnt["tC"]

            def v_sf(v, sf_done=sf_done):
                c.wait(v, "tC", sf_done)
                c.wait_buf(v, "SBUF")
                i = v.tensor_copy(SFL[:, :], psB[0:1, 0:T])
                c.inc(i, "vC")
            block.vector(v_sf)
            sfl_done = c.cnt["vC"]

            def t_sb(t_, sfl_done=sfl_done):
                c.wait(t_, "vC", sfl_done)
                for q in range(NQC):
                    i = t_.matmul(psB[:, q * 512:(q + 1) * 512], ONESF[0:1, :],
                                  SFL[:, q * 512:(q + 1) * 512], start=True, stop=True)
                c.inc(i, "tC")
            block.tensor(t_sb)
            sb_done = c.cnt["tC"]

            def v_sb(v, sb_done=sb_done):
                c.wait(v, "tC", sb_done)
                i = v.tensor_copy(SIGB[:, :], psB[:, :])
                c.inc(i, "vC")
            block.vector(v_sb)
            sigb_done = c.cnt["vC"]

            for cc in range(16):
                def v_g(v, cc=cc, sigb_done=sigb_done):
                    if cc == 0:
                        c.wait(v, "vC", sigb_done)
                        c.wait_buf(v, "SCB")
                    i = v.tensor_scalar(SCB[:, :], SIGB[:, :], SIG[:, cc:cc + 1],
                                        None, ALU.is_gt, ALU.add,
                                        accum_out=SV[:, 37 + cc:38 + cc])
                    c.fence(v, "vC", i)
                block.vector(v_g)
            g_done = c.cnt["vC"]
            c.use("SCB", "vC", g_done)

            def v_t1(v, g_done=g_done):
                c.wait(v, "vC", g_done)
                i = v.tensor_scalar(SV[:, 53:69], SIG[:, :], 1.0, None, ALU.is_ge)
                c.inc(i, "vC")
            block.vector(v_t1)
            om_done = c.cnt["vC"]

            def t_t(t_, om_done=om_done):
                c.wait(t_, "vC", om_done)
                i = t_.matmul(psC2[:, :], ONESF[:, :], SV[:, 53:69], start=True, stop=True)
                c.inc(i, "tC")
                i = t_.matmul(psP2[:, :], LTRI[:, :], SV[:, 53:69], start=True, stop=True)
                c.inc(i, "tC")
            block.tensor(t_t)
            tp_done = c.cnt["tC"]

            def v_rank(v, tp_done=tp_done):
                c.wait(v, "tC", tp_done)
                i = v.tensor_copy(SV[:, 69:85], psC2[:, :])
                c.fence(v, "vC", i)
                a, b2 = 69, 85
                for sh in (1, 2, 4, 8):
                    i = v.tensor_copy(SV[:, b2:b2 + sh], SV[:, a:a + sh])
                    c.fence(v, "vC", i)
                    i = v.tensor_tensor(SV[:, b2 + sh:b2 + 16], SV[:, a + sh:a + 16],
                                        SV[:, a:a + 16 - sh], ALU.add)
                    c.fence(v, "vC", i)
                    a, b2 = b2, a
                i = v.tensor_tensor(SV[:, a:a + 16], SV[:, a:a + 16], psC2[:, :],
                                    ALU.subtract)
                c.fence(v, "vC", i)
                i = v.tensor_tensor(SV[:, 101:117], SV[:, a:a + 16], psP2[:, :], ALU.add)
                c.fence(v, "vC", i)
                i = v.tensor_tensor(SV[:, 101:117], SV[:, 101:117], SV[:, 53:69],
                                    ALU.mult)
                c.fence(v, "vC", i)
                i = v.tensor_tensor(SV[:, 101:117], SV[:, 101:117], SV[:, 37:53],
                                    ALU.add)
                c.fence(v, "vC", i)
                i = v.tensor_scalar(SV[:, 101:117], SV[:, 101:117], float(N_HEAVY),
                                    None, ALU.is_lt)
                c.inc(i, "vC")
            block.vector(v_rank)
            rank_done = c.cnt["vC"]

            def t_mf(t_, rank_done=rank_done):
                c.wait(t_, "vC", rank_done)
                for cc in range(16):
                    i = t_.transpose(psB[0:1, cc * 128:(cc + 1) * 128],
                                     SV[:, 101 + cc:102 + cc], IDN[:, :])
                c.inc(i, "tC")
            block.tensor(t_mf)
            mf_done = c.cnt["tC"]

            def v_mf(v, mf_done=mf_done):
                c.wait(v, "tC", mf_done)
                i = v.tensor_copy(SFL[:, :], psB[0:1, 0:T])
                c.inc(i, "vC")
            block.vector(v_mf)
            mfl_done = c.cnt["vC"]

            def t_mb(t_, mfl_done=mfl_done):
                c.wait(t_, "vC", mfl_done)
                for q in range(NQC):
                    i = t_.matmul(psB[:, q * 512:(q + 1) * 512], ONESF[0:1, :],
                                  SFL[:, q * 512:(q + 1) * 512], start=True, stop=True)
                c.inc(i, "tC")
            block.tensor(t_mb)
            mb_done = c.cnt["tC"]

            def v_mb(v, mb_done=mb_done):
                c.wait(v, "tC", mb_done)
                i = v.tensor_copy(MASKB[:, :], psB[:, :])
                c.inc(i, "vC")
            block.vector(v_mb)
            mask_done = c.cnt["vC"]
            c.use("PSUM", "vC", mask_done)
            h2.close()
            c.sbuf_epoch()

            h3 = contextlib.ExitStack()
            HCH = h3.enter_context(nc.sbuf_tensor("HCH", [128, 32 * 512], BF16))
            WH2 = h3.enter_context(nc.sbuf_tensor("WH2", [128, 8 * 32 * 128], BF16))
            TMPC = h3.enter_context(nc.sbuf_tensor("TMPC", [128, 512], F32))
            psH2 = h3.enter_context(nc.psum_tensor("psH2", [128, 8 * 512], F32))

            wv_prev = c.cnt["qWV"]
            def g_wh2(g, wv_prev=wv_prev):
                c.wait_buf(g, "SBUF")
                c.wait(g, "qWV", wv_prev)
                for mt in range(DT):
                    qn = "qF0" if mt < 4 else "qWV"
                    c.dma(g, qn,
                          WH2[:, mt * 32 * 128:(mt + 1) * 32 * 128].rearrange(
                              "p (k cc) -> p k cc", cc=128),
                          dram["w2hT_blk"][mt * 4096:(mt + 1) * 4096, :].rearrange(
                              "(k p) cc -> p k cc", p=128))
            block.gpsimd(g_wh2)
            wh2_ready = (c.cnt["qF0"], c.cnt["qWV"])

            qk_prev = (c.cnt["qWQK0"], c.cnt["qWQK1"])
            QNS = ["qF1", "qF3", "qWQK0", "qWQK1"]
            for ch in range(NQC):
              for hf in range(2):
                u = ch * 2 + hf
                sl = u % 2
                qn = QNS[u % 4]
                def g_hch(g, u=u, ch=ch, hf=hf, sl=sl, qn=qn,
                          heavy1_done_q=heavy1_done_q, qk_prev=qk_prev):
                    if u == 0:
                        c.wait(g, "qF2", heavy1_done_q[0])
                        c.wait(g, "qF2b", heavy1_done_q[1])
                        c.wait_buf(g, "SBUF")
                    if u == 2:
                        c.wait(g, "qWQK0", qk_prev[0])
                    if u == 3:
                        c.wait(g, "qWQK1", qk_prev[1])
                    c.wait_buf(g, f"HCH{sl}")
                    for k in range(16):
                        kk = hf * 16 + k
                        c.dma(g, qn, HCH[:, (sl * 16 + k) * 512:(sl * 16 + k + 1) * 512],
                              Hdram[kk * 128:(kk + 1) * 128, ch * 512:(ch + 1) * 512])
                block.gpsimd(g_hch)
                hch_rdy = (qn, c.cnt[qn])

                def t_h2(t_, u=u, ch=ch, hf=hf, sl=sl, hch_rdy=hch_rdy,
                         wh2_ready=wh2_ready, mask_done=mask_done):
                    c.wait(t_, hch_rdy[0], hch_rdy[1])
                    if u == 0:
                        c.wait(t_, "qF0", wh2_ready[0])
                        c.wait(t_, "qWV", wh2_ready[1])
                        c.wait(t_, "vC", mask_done)
                    if hf == 0:
                        c.wait_buf(t_, "psH2")
                    for k in range(16):
                        kk = hf * 16 + k
                        for mt in range(DT):
                            i = t_.matmul(
                                psH2[:, mt * 512:(mt + 1) * 512],
                                WH2[:, (mt * 32 + kk) * 128:(mt * 32 + kk + 1) * 128],
                                HCH[:, (sl * 16 + k) * 512:(sl * 16 + k + 1) * 512],
                                start=(kk == 0), stop=(kk == 31))
                    c.inc(i, "tC")
                block.tensor(t_h2)
                h2_done = c.cnt["tC"]
                c.use(f"HCH{sl}", "tC", h2_done)

                if hf == 1:
                    for mt in range(DT):
                        def v_h2(v, ch=ch, mt=mt, h2_done=h2_done):
                            if mt == 0:
                                c.wait(v, "tC", h2_done)
                            i = v.scalar_tensor_tensor(
                                TMPC[:, :], psH2[:, mt * 512:(mt + 1) * 512],
                                SMALL[:, 32 + mt:33 + mt],
                                MASKB[:, ch * 512:(ch + 1) * 512],
                                ALU.add, ALU.mult)
                            c.fence(v, "vC", i)
                            i = v.tensor_tensor(ap_x(mt, ch * 512, (ch + 1) * 512),
                                                ap_x(mt, ch * 512, (ch + 1) * 512),
                                                TMPC[:, :], ALU.add)
                            c.fence(v, "vC", i)
                        block.vector(v_h2)
                    c.use("psH2", "vC", c.cnt["vC"])
            ffn_done = c.cnt["vC"]
            c.use("PSUM", "vC", ffn_done)
            h3.close()
            c.sbuf_epoch()
            return ffn_done


        def ffn1():
            scA = contextlib.ExitStack()
            LNX4 = scA.enter_context(nc.sbuf_tensor("LNX4", [128, DT * T], BF16))

            def ap_lnx(t, c0=0, c1=T):
                return LNX4[:, t * T + c0: t * T + c1]

            ln_done = layernorm("ln4", LNX4)

            PVW = scA.enter_context(nc.sbuf_tensor("PVW", [128, 32 * 512], BF16))
            OIWB = scA.enter_context(nc.sbuf_tensor("OIWB", [128, 4 * 1024], BF16))
            OPS = scA.enter_context(nc.sbuf_tensor("OPS", [128, 8 * 1024], BF16))
            PIWB = scA.enter_context(nc.sbuf_tensor("PIWB", [128, 2 * 512], BF16))
            WIN = scA.enter_context(nc.sbuf_tensor("WIN", [128, 512], F32))
            OHR = scA.enter_context(nc.sbuf_tensor("OHR", [128, 8], F32))
            IRT = scA.enter_context(nc.sbuf_tensor("IRT", [128, 512], BF16))
            GJ = scA.enter_context(nc.sbuf_tensor("GJ", [128, T], BF16))
            WIP = scA.enter_context(nc.sbuf_tensor("WIP", [128, 2 * DT * 128], BF16))
            ZB8 = scA.enter_context(nc.sbuf_tensor("ZB8", [8, 512], F32))
            PC = scA.enter_context(nc.sbuf_tensor("PC", [8, 512], F32))
            PA = scA.enter_context(nc.sbuf_tensor("PA", [128, 256], F32))
            E2 = scA.enter_context(nc.sbuf_tensor("E2", [128, 256], F32))
            W2F = scA.enter_context(nc.sbuf_tensor("W2F", [128, 256], F32))
            TRB = scA.enter_context(nc.sbuf_tensor("TRB", [128, 256], BF16))
            AGGT = scA.enter_context(nc.sbuf_tensor("AGGT", [128, 32], BF16))
            AO = scA.enter_context(nc.sbuf_tensor("AO", [128, 512], F32))
            TRO = scA.enter_context(nc.sbuf_tensor("TRO", [128, 512], BF16))
            OSP = scA.enter_context(nc.sbuf_tensor("OSP", [8, 1024], F32))
            OSB = scA.enter_context(nc.sbuf_tensor("OSB", [8, 1024], F32))
            AOC = OSP

            def g_small(g):
                c.wait_buf(g, "SBUF")
                for k in range(32):
                    c.dma(g, "qF3", PVW[:, k * 512:(k + 1) * 512],
                          dram["pv"][k * 128:(k + 1) * 128, :])
                for k in range(4):
                    c.dma(g, "qF3", OIWB[:, k * 1024:(k + 1) * 1024],
                          dram["oiw_shT"][k * 128:(k + 1) * 128, :])
                for k in range(DT):
                    c.dma(g, "qF3", OPS[:, k * 1024:(k + 1) * 1024],
                          dram["op_sh"][k * 128:(k + 1) * 128, :])
                c.dma(g, "qF3", WIN[:, :], dram["winm"])
                c.dma(g, "qF3", OHR[:, :], dram["ohrow"])
            block.gpsimd(g_small)
            small_ready = c.cnt["qF3"]

            def v_zb(v):
                i = v.memset(ZB8[:, :], 0.0)
                c.inc(i, "vC")
            block.vector(v_zb)
            zb8_done = c.cnt["vC"]

            def g_zero(g, zb8_done=zb8_done):
                c.wait(g, "vC", zb8_done)
                c.wait(g, "qWQK0", c.cnt["qWQK0"])
                c.wait(g, "qWQK1", c.cnt["qWQK1"])
                for j in range(8):
                    c.dma(g, "qWQK0", ccd["paA"][0:8, j * 512:(j + 1) * 512],
                          ZB8[:, :])
                for j in range(16):
                    c.dma(g, "qWQK1", ccd["oaA"][0:8, j * 512:(j + 1) * 512],
                          ZB8[:, :])
            block.gpsimd(g_zero)
            zpa_done = c.cnt["qWQK0"]
            zoa_done = c.cnt["qWQK1"]

            bm = contextlib.ExitStack()
            psA4 = bm.enter_context(nc.psum_tensor("psA4", [128, 2 * T], F32))
            for n in range(64):
                buf = n % 2
                qw = "qF0" if buf == 0 else "qF1"

                def g_ip(g, n=n, buf=buf, qw=qw):
                    c.wait_buf(g, f"WIP{buf}")
                    if n < 2:
                        c.wait_buf(g, "SBUF")
                    c.dma(g, qw,
                          WIP[:, buf * DT * 128:(buf + 1) * DT * 128].rearrange(
                              "p (k cc) -> p k cc", cc=128),
                          dram["ipT_blk"][n * 1024:(n + 1) * 1024, :].rearrange(
                              "(k p) cc -> p k cc", p=128))
                block.gpsimd(g_ip)
                ip_ready = c.cnt[qw]

                def t_bm(t_, n=n, buf=buf, qw=qw, ip_ready=ip_ready):
                    c.wait(t_, qw, ip_ready)
                    if n == 0:
                        c.wait(t_, "vC", ln_done)
                        c.wait_buf(t_, "PSUM")
                    c.wait_buf(t_, f"psA4{buf}")
                    for q in range(NQC):
                        for k in range(DT):
                            i = t_.matmul(
                                psA4[:, buf * T + q * 512: buf * T + (q + 1) * 512],
                                WIP[:, (buf * DT + k) * 128:(buf * DT + k + 1) * 128],
                                ap_lnx(k, q * 512, (q + 1) * 512),
                                start=(k == 0), stop=(k == DT - 1))
                    c.inc(i, "tC")
                block.tensor(t_bm)
                bm_done = c.cnt["tC"]
                c.use(f"WIP{buf}", "tC", bm_done)

                bmv = c.reserve("sC")
                gj_w = c.snap("GJ")
                def s_bm(sc, n=n, buf=buf, bm_done=bm_done, gj_w=gj_w):
                    c.wait(sc, "tC", bm_done)
                    c.wait_list(sc, gj_w)
                    i = sc.activation(GJ[:, :],
                                      psA4[:, buf * T:(buf + 1) * T],
                                      AF.Gelu, accum_out=SV[:, 37 + n:38 + n])
                    c.inc_to(i, "sC")
                c.sc(s_bm)
                c.use(f"psA4{buf}", "sC", bmv)
                c.use("GJ", "sC", bmv)
            macc_done = c.cnt["sC"]
            c.use("PSUM", "sC", macc_done)
            bm.close()

            def bisect(BUFap, F, kval, lo0, hi0, iters, grp, tag, first_wait):
                def v_init(v):
                    c.wait(v, "vC", c.cnt["vC"])
                    c.wait(v, first_wait[0], first_wait[1])
                    i = v.memset(SV[:, 120:121], 0.5 * (lo0 + hi0))
                    c.fence(v, "vC", i)
                block.vector(v_init)

                ps_cn = contextlib.ExitStack()
                psCNT = ps_cn.enter_context(
                    nc.psum_tensor(f"psCNT{tag}", [128, 1], F32))
                for it in range(iters):
                    def v_b1(v, it=it):
                        c.wait_buf(v, "SCB")
                        i = v.tensor_scalar(SCB[:, 0:F], BUFap, SV[:, 120:121],
                                            None, ALU.is_ge, ALU.add,
                                            accum_out=SV[:, 121:122])
                        c.inc(i, "vC")
                    block.vector(v_b1)
                    pd = c.cnt["vC"]

                    def t_b(t_, pd=pd, it=it, tag=tag):
                        c.wait(t_, "vC", pd)
                        if it == 0:
                            c.wait_buf(t_, "PSUM")
                        c.wait_buf(t_, f"psCNT{tag}")
                        i = t_.matmul(psCNT[:, :], grp, SV[:, 121:122],
                                      start=True, stop=True)
                        c.inc(i, "tC")
                    block.tensor(t_b)
                    cd = c.cnt["tC"]

                    d = (hi0 - lo0) / 2.0 ** (it + 2)
                    def v_b2(v, cd=cd, d=d, tag=tag):
                        c.wait(v, "tC", cd)
                        i = v.tensor_scalar(SV[:, 124:125], psCNT[:, :],
                                            float(kval), 2.0 * d,
                                            ALU.is_ge, ALU.mult)
                        c.fence(v, "vC", i)
                        i = v.tensor_scalar(SV[:, 124:125], SV[:, 124:125],
                                            -d, None, ALU.add)
                        c.fence(v, "vC", i)
                        i = v.tensor_tensor(SV[:, 120:121], SV[:, 120:121],
                                            SV[:, 124:125], ALU.add)
                        c.fence(v, "vC", i)
                    block.vector(v_b2)
                    c.use(f"psCNT{tag}", "vC", c.cnt["vC"])
                    c.use("SCB", "vC", c.cnt["vC"])
                w = (hi0 - lo0) / 2.0 ** (iters + 1)
                def v_fin(v, w=w):
                    c.wait(v, "vC", c.cnt["vC"])
                    i = v.tensor_scalar(SV[:, 118:119], SV[:, 120:121],
                                        -1.0625 * w, None, ALU.add)
                    c.inc(i, "vC")
                block.vector(v_fin)
                th_done = c.cnt["vC"]
                c.use("PSUM", "vC", th_done)
                ps_cn.close()
                return th_done

            b1 = bisect(SV[:, 37:101], 64, N_IN // 8, -400.0, 50000.0, 36,
                        ONESF[:, :], "s1", ("sC", macc_done))

            def v_ir(v, b1=b1):
                c.wait(v, "vC", b1)
                c.wait_buf(v, "SCB")
                i = v.tensor_scalar(SCB[:, 0:64], SV[:, 37:101], SV[:, 118:119],
                                    None, ALU.is_ge)
                c.fence(v, "vC", i)
                i = v.tensor_tensor(SV[:, 37:101], SV[:, 37:101], SCB[:, 0:64],
                                    ALU.mult)
                c.fence(v, "vC", i)
                i = v.tensor_scalar(SV[:, 37:101], SV[:, 37:101], 1.0 / T,
                                    None, ALU.mult)
                c.fence(v, "vC", i)
                c.wait(v, "qF3", small_ready)
                irt3 = TRO[:, :].rearrange("p (t r) -> p t r", r=8)
                for r in range(8):
                    i = v.tensor_scalar(irt3[:, :, r], SV[:, 37:101],
                                        OHR[:, r:r + 1], None, ALU.mult)
                c.inc(i, "vC")
            block.vector(v_ir)
            irt_done = c.cnt["vC"]
            c.use("SCB", "vC", irt_done)

            def g_cc1(g, irt_done=irt_done):
                c.wait(g, "vC", irt_done)
                c.dma(g, "qF0", ccd["irA"], TRO[:, :])
                c.wait(g, "qF0", c.cnt["qF0"])
                i = g.collective_compute(
                    "AllReduce", ALU.add, replica_groups=[list(range(8))],
                    ins=[ccd["irA"]], outs=[ccd["irB"]])
                c.inc(i, "gC")
                c.wait(g, "gC", c.cnt["gC"])
                c.dma(g, "qF0", IRT[:, :], ccd["irB"])
            block.gpsimd(g_cc1)
            irt_ready = c.cnt["qF0"]

            pr = contextlib.ExitStack()
            psPR = pr.enter_context(nc.psum_tensor("psPR", [8, 512], F32))
            for k in range(64):
                buf = k % 2
                qw = "qF1" if buf == 0 else "qF2"

                def g_piw(g, k=k, buf=buf, qw=qw):
                    c.wait_buf(g, f"PIWB{buf}")
                    if k < 2:
                        c.wait(g, "qF0", irt_ready)
                    c.dma(g, qw, PIWB[:, buf * 512:(buf + 1) * 512],
                          dram["piwT_sh"][k * 128:(k + 1) * 128, :])
                block.gpsimd(g_piw)
                piw_ready = c.cnt[qw]

                def t_pr(t_, k=k, buf=buf, qw=qw, piw_ready=piw_ready,
                         irt_ready=irt_ready):
                    c.wait(t_, qw, piw_ready)
                    if k == 0:
                        c.wait(t_, "qF0", irt_ready)
                        c.wait_buf(t_, "PSUM")
                    i = t_.matmul(psPR[:, :], IRT[:, k * 8:(k + 1) * 8],
                                  PIWB[:, buf * 512:(buf + 1) * 512],
                                  start=(k == 0), stop=(k == 63))
                    c.inc(i, "tC")
                block.tensor(t_pr)
                c.use(f"PIWB{buf}", "tC", c.cnt["tC"])
            pr_done = c.cnt["tC"]

            def v_pc(v, pr_done=pr_done):
                c.wait(v, "tC", pr_done)
                i = v.tensor_copy(PC[:, :], psPR[:, :])
                c.inc(i, "vC")
            block.vector(v_pc)
            pc_done = c.cnt["vC"]
            c.use("PSUM", "vC", pc_done)
            pr.close()

            def g_cc2(g, pc_done=pc_done, zpa_done=zpa_done):
                c.wait(g, "vC", pc_done)
                c.wait(g, "qWQK0", zpa_done)
                c.wait(g, "qF1", c.cnt["qF1"])
                with g.register() as r:
                    pid = g.partition_id()
                    g.reg_alu(r, pid, 512, ALU.mult)
                    dst = bass.AP(ccd_h["paA"], r, [[N_PROC, 8], [1, 512]])
                    c.dma(g, "qF1", dst, PC[:, :])
                c.wait(g, "qF1", c.cnt["qF1"])
                i = g.collective_compute(
                    "AllReduce", ALU.add, replica_groups=[list(range(8))],
                    ins=[ccd["paA"]], outs=[ccd["paB"]])
                c.inc(i, "gC")
                c.wait(g, "gC", c.cnt["gC"])
                c.dma(g, "qF1", PA[:, :],
                      ccd["paB"].rearrange("r (g f) -> (r g) f", f=256))
            block.gpsimd(g_cc2)
            pa_ready = c.cnt["qF1"]

            pa_done = c.reserve("sC")
            def s_pa(sc, pa_ready=pa_ready):
                c.wait(sc, "qF1", pa_ready)
                i = sc.activation(PA[:, :], PA[:, :], AF.Gelu)
                c.inc_to(i, "sC")
            c.sc(s_pa)

            b2 = bisect(PA[:, :], 256, N_PROC // 8, -1.0, 64.0, 30,
                        BLKF[:, :], "s2", ("sC", pa_done))

            e2_done = c.reserve("sC")
            def s_e2(sc, b2=b2):
                c.wait(sc, "vC", b2)
                i = sc.activation(E2[:, :], PA[:, :], AF.Exp)
                c.inc_to(i, "sC")
            c.sc(s_e2)

            sm = contextlib.ExitStack()
            psSM = sm.enter_context(nc.psum_tensor("psSM", [128, 1], F32))
            psTR = sm.enter_context(nc.psum_tensor("psTR", [128, 256], F32))
            psAG = sm.enter_context(nc.psum_tensor("psAG", [128, 32], F32))

            def v_sm(v, e2_done=e2_done):
                c.wait(v, "sC", e2_done)
                c.wait_buf(v, "SCB")
                i = v.tensor_scalar(SCB[:, 0:256], PA[:, :], SV[:, 118:119],
                                    None, ALU.is_ge)
                c.fence(v, "vC", i)
                i = v.tensor_tensor(E2[:, :], E2[:, :], SCB[:, 0:256], ALU.mult)
                c.fence(v, "vC", i)
                i = v.reduce_sum(SV[:, 121:122], E2[:, :], mybir.AxisListType.X)
                c.inc(i, "vC")
            block.vector(v_sm)
            sm1 = c.cnt["vC"]
            c.use("SCB", "vC", sm1)

            def t_sm(t_, sm1=sm1):
                c.wait(t_, "vC", sm1)
                c.wait_buf(t_, "PSUM")
                i = t_.matmul(psSM[:, :], BLKF[:, :], SV[:, 121:122],
                              start=True, stop=True)
                c.inc(i, "tC")
            block.tensor(t_sm)
            sm2 = c.cnt["tC"]

            def v_w2(v, sm2=sm2):
                c.wait(v, "tC", sm2)
                i = v.reciprocal(SV[:, 123:124], psSM[:, :])
                c.fence(v, "vC", i)
                i = v.tensor_scalar(W2F[:, :], E2[:, :], SV[:, 123:124],
                                    None, ALU.mult)
                c.inc(i, "vC")
            block.vector(v_w2)
            w2_done = c.cnt["vC"]

            def t_wt(t_, w2_done=w2_done):
                c.wait(t_, "vC", w2_done)
                for fh in range(2):
                    i = t_.transpose(psTR[:, fh * 128:(fh + 1) * 128],
                                     W2F[:, fh * 128:(fh + 1) * 128], IDN[:, :])
                c.inc(i, "tC")
            block.tensor(t_wt)
            wt_done = c.cnt["tC"]

            trb_done = c.reserve("sC")
            def s_trb(sc, wt_done=wt_done):
                c.wait(sc, "tC", wt_done)
                i = sc.copy(TRB[:, :], psTR[:, :])
                c.inc_to(i, "sC")
            c.sc(s_trb)

            trr = TRB[:, :].rearrange("p (fh r g) -> p fh g r", fh=2, r=8, g=16)

            def t_ag(t_, trb_done=trb_done, small_ready=small_ready):
                c.wait(t_, "sC", trb_done)
                c.wait(t_, "qF3", small_ready)
                for mtile in range(4):
                    for pt in range(32):
                        g_ = pt // 2
                        fh = pt % 2
                        i = t_.matmul(psAG[:, mtile * 8:(mtile + 1) * 8],
                                      PVW[:, pt * 512 + mtile * 128: pt * 512 + (mtile + 1) * 128],
                                      trr[:, fh, g_, :],
                                      start=(pt == 0), stop=(pt == 31))
                c.inc(i, "tC")
            block.tensor(t_ag)
            ag_done = c.cnt["tC"]

            aggt_done = c.reserve("sC")
            def s_ag(sc, ag_done=ag_done):
                c.wait(sc, "tC", ag_done)
                i = sc.copy(AGGT[:, :], psAG[:, :])
                c.inc_to(i, "sC")
            c.sc(s_ag)
            c.use("PSUM", "sC", aggt_done)
            sm.close()

            ao = contextlib.ExitStack()
            psAO = ao.enter_context(nc.psum_tensor("psAO", [8, 1024], F32))
            def t_ao(t_, aggt_done=aggt_done):
                c.wait(t_, "sC", aggt_done)
                c.wait_buf(t_, "PSUM")
                for nch in range(2):
                    for k in range(4):
                        i = t_.matmul(psAO[:, nch * 512:(nch + 1) * 512],
                                      AGGT[:, k * 8:(k + 1) * 8],
                                      OIWB[:, k * 1024 + nch * 512: k * 1024 + (nch + 1) * 512],
                                      start=(k == 0), stop=(k == 3))
                c.inc(i, "tC")
            block.tensor(t_ao)
            ao_done = c.cnt["tC"]

            aoc_done = c.reserve("sC")
            def s_ao(sc, ao_done=ao_done):
                c.wait(sc, "tC", ao_done)
                i = sc.activation(AOC[:, :], psAO[:, :], AF.Gelu)
                c.inc_to(i, "sC")
            c.sc(s_ao)
            c.use("PSUM", "sC", aoc_done)
            ao.close()

            def g_cc3(g, aoc_done=aoc_done, zoa_done=zoa_done):
                c.wait(g, "sC", aoc_done)
                c.wait(g, "qWQK1", zoa_done)
                c.wait(g, "qF2", c.cnt["qF2"])
                with g.register() as r:
                    pid = g.partition_id()
                    g.reg_alu(r, pid, 1024, ALU.mult)
                    dst = bass.AP(ccd_h["oaA"], r, [[N_OUT, 8], [1, 1024]])
                    c.dma(g, "qF2", dst, AOC[:, :])
                c.wait(g, "qF2", c.cnt["qF2"])
                i = g.collective_compute(
                    "AllReduce", ALU.add, replica_groups=[list(range(8))],
                    ins=[ccd["oaA"]], outs=[ccd["oaB"]])
                c.inc(i, "gC")
                c.wait(g, "gC", c.cnt["gC"])
                c.dma(g, "qF2", AO[:, :],
                      ccd["oaB"].rearrange("r (g f) -> (r g) f", f=512))
            block.gpsimd(g_cc3)
            ao_ready = ("qF2", c.cnt["qF2"])

            b3 = bisect(AO[:, :], 512, N_OUT // 8, -1.0, 64.0, 30,
                        BLKF[:, :], "s3", ao_ready)

            def v_m3(v, b3=b3):
                c.wait(v, "vC", b3)
                c.wait_buf(v, "SCB")
                i = v.tensor_scalar(SCB[:, 0:512], AO[:, :], SV[:, 118:119],
                                    None, ALU.is_ge)
                c.fence(v, "vC", i)
                i = v.tensor_tensor(AO[:, :], AO[:, :], SCB[:, 0:512], ALU.mult)
                c.fence(v, "vC", i)
                i = v.tensor_tensor(AO[:, :], AO[:, :], WIN[:, :], ALU.mult)
                c.inc(i, "vC")
            block.vector(v_m3)
            m3_done = c.cnt["vC"]
            c.use("SCB", "vC", m3_done)

            fo = contextlib.ExitStack()
            psTRO = fo.enter_context(nc.psum_tensor("psTRO", [128, 512], F32))
            psOS = fo.enter_context(nc.psum_tensor("psOS", [8, 1024], F32))
            psROW = fo.enter_context(nc.psum_tensor("psROW", [1, 1024], F32))
            psORT = fo.enter_context(nc.psum_tensor("psORT", [128, 8], F32))

            def t_tro(t_, m3_done=m3_done):
                c.wait(t_, "vC", m3_done)
                c.wait_buf(t_, "PSUM")
                for fh in range(4):
                    i = t_.transpose(psTRO[:, fh * 128:(fh + 1) * 128],
                                     AO[:, fh * 128:(fh + 1) * 128], IDN[:, :])
                c.inc(i, "tC")
            block.tensor(t_tro)
            tro_done = c.cnt["tC"]

            trod = c.reserve("sC")
            def s_tro(sc, tro_done=tro_done):
                c.wait(sc, "tC", tro_done)
                i = sc.copy(TRO[:, :], psTRO[:, :])
                c.inc_to(i, "sC")
            c.sc(s_tro)

            tror = TRO[:, :].rearrange("p (fh r g) -> p fh g r", fh=4, r=8, g=16)

            def t_os(t_, trod=trod):
                c.wait(t_, "sC", trod)
                for nch in range(2):
                    for j in range(64):
                        g_ = j // 4
                        fh = j % 4
                        i = t_.matmul(psOS[:, nch * 512:(nch + 1) * 512],
                                      tror[:, fh, g_, :],
                                      OPS[:, (j % 8) * 1024 + nch * 512: (j % 8) * 1024 + (nch + 1) * 512],
                                      start=(j == 0), stop=(j == 63))
                c.inc(i, "tC")
            block.tensor(t_os)
            os_done = c.cnt["tC"]

            def v_os(v, os_done=os_done):
                c.wait(v, "tC", os_done)
                i = v.tensor_copy(OSP[:, :], psOS[:, :])
                c.inc(i, "vC")
            block.vector(v_os)
            osp_done = c.cnt["vC"]

            def g_cc4(g, osp_done=osp_done):
                c.wait(g, "vC", osp_done)
                c.dma(g, "qF3", ccd["osA"], OSP[:, :])
                c.wait(g, "qF3", c.cnt["qF3"])
                i = g.collective_compute(
                    "AllReduce", ALU.add, replica_groups=[list(range(8))],
                    ins=[ccd["osA"]], outs=[ccd["osB"]])
                c.inc(i, "gC")
                c.wait(g, "gC", c.cnt["gC"])
                c.dma(g, "qF3", OSB[:, :], ccd["osB"])
            block.gpsimd(g_cc4)
            osb_ready = c.cnt["qF3"]

            def t_row(t_, osb_ready=osb_ready):
                c.wait(t_, "qF3", osb_ready)
                for nch in range(2):
                    i = t_.matmul(psROW[:, nch * 512:(nch + 1) * 512],
                                  SMALL[0:8, 76:77],
                                  OSB[:, nch * 512:(nch + 1) * 512],
                                  start=True, stop=True)
                c.inc(i, "tC")
            block.tensor(t_row)
            row_done = c.cnt["tC"]

            def v_row(v, row_done=row_done):
                c.wait(v, "tC", row_done)
                i = v.tensor_copy(OSP[0:1, :], psROW[:, :])
                c.inc(i, "vC")
            block.vector(v_row)
            orw_done = c.cnt["vC"]

            def t_ort(t_, orw_done=orw_done):
                c.wait(t_, "vC", orw_done)
                for t in range(DT):
                    i = t_.transpose(psORT[:, t:t + 1],
                                     OSP[0:1, t * 128:(t + 1) * 128], IDN[0:1, 0:1])
                c.inc(i, "tC")
            block.tensor(t_ort)
            ort_done = c.cnt["tC"]

            fin_base = c.cnt["vC"]
            def v_fin(v, ort_done=ort_done):
                c.wait(v, "tC", ort_done)
                i = v.tensor_copy(SV[:, 104:112], psORT[:, :])
                c.fence(v, "vC", i)
                for t in range(DT):
                    i = v.tensor_scalar(ap_x(t), ap_x(t), SV[:, 104 + t:105 + t],
                                        None, ALU.add)
                    c.fence(v, "vC", i)
            block.vector(v_fin)
            fin = c.cnt["vC"]
            fin_tiles = [fin_base + 2 + t for t in range(DT)]
            c.use("PSUM", "vC", fin)
            fo.close()
            scA.close()
            c.sbuf_epoch()
            return fin, fin_tiles

        def store_x(after_v, tiles=None):
            c.sc_flush()
            qs = ["qX", "qX1", "qX2", "qX3"]
            def st(s):
                for qn in qs:
                    c.wait(s, qn, c.cnt[qn])
                if tiles is None:
                    c.wait(s, "vC", after_v)
                for t in range(DT):
                    if tiles is not None:
                        c.wait(s, "vC", tiles[t])
                    c.dma(s, qs[t % 4], out[t * 128:(t + 1) * 128, :], ap_x(t))
                for qn in qs:
                    c.wait(s, qn, c.cnt[qn])
            block.sync(st)

        lnx_scope = contextlib.ExitStack()
        LNX0 = lnx_scope.enter_context(nc.sbuf_tensor("LNX0", [128, DT * T], BF16))
        ln1 = layernorm("ln1", LNX0, first=True)
        r1 = attention(0, LNX0, ln1)
        lnx_scope.close()

        if debug_stage == "x1":
            store_x(r1)

            return nc, dram

        r2 = ffn0()
        if debug_stage == "sig":
            nc._keep = c
            return nc, dram
        if debug_stage == "x2":
            store_x(r2)

            return nc, dram

        lnx_scope2 = contextlib.ExitStack()
        LNX1 = lnx_scope2.enter_context(nc.sbuf_tensor("LNX1", [128, DT * T], BF16))
        ln3 = layernorm("ln3", LNX1)
        r3 = attention(1, LNX1, ln3)
        lnx_scope2.close()

        if debug_stage == "x3":
            store_x(r3)

            return nc, dram

        r4, r4_tiles = ffn1()
        store_x(r4, tiles=r4_tiles)


    return nc, dram



def rope_tables():
    inv = (1.0 / 10000.0 ** (np.arange(0, HD, 2, dtype=np.float64) / HD))
    t = np.arange(T, dtype=np.float64)
    ang = np.outer(inv, t)
    cos32 = np.cos(ang); sin32 = np.sin(ang)
    COS = np.empty((128, T), dtype=np.float32)
    SIN = np.empty((128, T), dtype=np.float32)
    for p_ in range(128):
        j = p_ % 32
        COS[p_] = cos32[j]
        SIN[p_] = (sin32[j] if (p_ % 64) >= 32 else -sin32[j])
    return COS, SIN


def deinterleave_rows(w):
    perm = np.empty(D, dtype=np.int64)
    for h in range(H):
        b0 = h * HD
        perm[b0:b0 + 32] = b0 + np.arange(0, HD, 2)
        perm[b0 + 32:b0 + 64] = b0 + np.arange(1, HD, 2)
    return w[perm]


def make_inputs(inp, core):
    x = np.ascontiguousarray(np.asarray(inp["L0_attn_x"])[core].T)
    COS, SIN = rope_tables()
    onehot = np.zeros((8, 1), np.float32); onehot[core, 0] = 1.0
    ident = np.eye(128, dtype=np.float32)
    ltri = np.triu(np.ones((128, 128), np.float32), 1)
    winm = np.zeros((128, 512), np.float32)
    for p_ in range(128):
        g_ = p_ % 16
        lo_, hi_ = core * 1024, (core + 1) * 1024
        n0 = 512 * g_
        s0, s1 = max(lo_ - n0, 0), min(hi_ - n0, 512)
        if s0 < s1:
            winm[p_, s0:s1] = 1.0
    ones128 = np.ones((128, 128), np.float32)
    blk = np.zeros((128, 128), np.float32)
    for g in range(8):
        blk[g * 16:(g + 1) * 16, g * 16:(g + 1) * 16] = 1.0

    def vf(vname):
        v = np.asarray(inp[vname])
        return np.ascontiguousarray(v.reshape(-1, 128).T)

    a = np.asarray
    d = {
        "xT": x, "rope_cos": COS, "rope_sin": SIN,
        "onehot": onehot, "ones128": ones128, "blkones": blk,
        "ident": ident, "ltri": ltri,
        "a0_wqT": a(inp["L0_attn_wq"]).T, "a0_wkT": a(inp["L0_attn_wk"]).T,
        "a0_wvT": a(inp["L0_attn_wv"]).T, "a0_woT": a(inp["L0_attn_wo"]).T,
        "a1_wqT": deinterleave_rows(a(inp["L1_attn_Wq"])).T,
        "a1_wkT": deinterleave_rows(a(inp["L1_attn_Wk"])).T,
        "a1_wvT": a(inp["L1_attn_Wv"]).T, "a1_woT": a(inp["L1_attn_Wo"]).T,
        "rt": vf("L0_ffn_routing_token"),
        "gl": vf("L0_ffn_gamma_l"), "gh": vf("L0_ffn_gamma_h"),
        "b2l": vf("L0_ffn_b2_l"), "b2h": vf("L0_ffn_b2_h"),
        "b1l": vf("L0_ffn_b1_l"), "b1h": vf("L0_ffn_b1_h"),
        "w1lT": a(inp["L0_ffn_w1_l"]).T, "w2lT": a(inp["L0_ffn_w2_l"]).T,
        "w1hT_blk": a(inp["L0_ffn_w1_h"]).T.reshape(D, 32, 128).transpose(1, 0, 2).reshape(32 * D, 128),
        "w2hT_blk": a(inp["L0_ffn_w2_h"]).T.reshape(4096, 8, 128).transpose(1, 0, 2).reshape(8 * 4096, 128),
        "ipT_blk": a(inp["L1_ffn_input_patterns"]).T.reshape(D, 64, 128).transpose(1, 0, 2).reshape(64 * D, 128),
        "piwT_sh": a(inp["L1_ffn_process_input_weights"]).T[:, core * 512:(core + 1) * 512],
        "pv": a(inp["L1_ffn_process_values"]),
        "oiw_shT": a(inp["L1_ffn_output_input_weights"])[core * 1024:(core + 1) * 1024, :].T,
        "op_sh": a(inp["L1_ffn_output_patterns"])[core * 1024:(core + 1) * 1024, :],
        "ohrow": np.tile(onehot.reshape(1, 8), (128, 1)),
        "winm": winm,
    }
    return {k: np.ascontiguousarray(v, dtype=np.float32) for k, v in d.items()}


_NC_CACHE = {}


def kernel(**inputs):
    key = DEBUG_STAGE
    if key not in _NC_CACHE:
        _NC_CACHE[key] = build_program(debug_stage=DEBUG_STAGE)
    nc, _ = _NC_CACHE[key]
    in_maps = [make_inputs(inputs, core) for core in range(8)]
    res = run_bass_kernel_spmd(nc, in_maps, core_ids=list(range(8)))
    outv = np.empty((B, T, D), dtype=np.float32)
    for core in range(8):
        outv[core] = res.results[core]["outT"].T
    return outv


# version 13 (final)
# speedup vs baseline: 1.0115x; 1.0115x over previous
import contextlib
import numpy as np
import concourse.bass as bass
import concourse.mybir as mybir
from concourse.bass_utils import run_bass_kernel_spmd

F32 = mybir.dt.float32
BF16 = mybir.dt.bfloat16
AF = mybir.ActivationFunctionType
ALU = mybir.AluOpType

B, T, D = 8, 2048, 1024
H, HD = 16, 64
NKT = T // 128
NQC = T // 512
DT = D // 128
N_HEAVY = 1024
COOR_K = 9.0 / 8.0 * N_HEAVY
CAP = 30.0
N_IN, N_PROC, N_OUT, D_PV = 8192, 4096, 8192, 512

DEBUG_STAGE = None


class Ctx:
    def __init__(self):
        self.nc = bass.Bass("TRN2", target_bir_lowering=False)
        self.es = contextlib.ExitStack()
        self.block = None
        self.sems = {}
        self.cnt = {}
        self.guard = {}
        self.scq = []

    def new_sem(self, name):
        self.sems[name] = self.es.enter_context(self.nc.semaphore(name))
        self.cnt[name] = 0

    def inc(self, inst, sem, n=1):
        inst.then_inc(self.sems[sem], n)
        self.cnt[sem] += n
        return self.cnt[sem]

    def dma(self, eng, sem, out, in_):
        i = eng.dma_start(out=out, in_=in_)
        return self.inc(i, sem, 16)

    def wait(self, eng, sem, val):
        if val > 0:
            eng.wait_ge(self.sems[sem], val)

    def use(self, buf, sem, val):
        self.guard[buf] = [(sem, val)]

    def add_use(self, buf, sem, val):
        self.guard.setdefault(buf, []).append((sem, val))

    def wait_buf(self, eng, buf):
        for (s, v) in self.guard.get(buf, []):
            self.wait(eng, s, v)

    def reserve(self, sem, n=1):
        self.cnt[sem] += n
        return self.cnt[sem]

    def inc_to(self, inst, sem, n=1):
        inst.then_inc(self.sems[sem], n)

    def snap(self, buf):
        return list(self.guard.get(buf, []))

    def wait_list(self, eng, lst):
        for (s, v) in lst:
            self.wait(eng, s, v)

    def fence_to(self, eng, sem, inst, val):
        inst.then_inc(self.sems[sem], 1)
        eng.wait_ge(self.sems[sem], val)

    def sc(self, fn):
        self.scq.append(fn)

    def sc_flush(self):
        if self.scq:
            fns = self.scq
            self.scq = []
            self.block.scalar(lambda s: [f(s) for f in fns])

    def sbuf_epoch(self, extra=()):
        self.guard["SBUF"] = [(s, self.cnt[s]) for s in
                              ("tC", "vC", "sC", "gC", *extra)]

    def fence(self, eng, sem, inst):
        self.inc(inst, sem)
        self.wait(eng, sem, self.cnt[sem])


def build_program(debug_stage=None):
    c = Ctx()
    nc = c.nc
    es = c.es
    dram = {}

    def din(name, shape):
        dram[name] = nc.dram_tensor(name, list(shape), F32, kind="ExternalInput").ap()
        return dram[name]

    din("xT", (D, T))
    din("rope_cos", (128, T)); din("rope_sin", (128, T))
    din("onehot", (8, 1)); din("ones128", (128, 128)); din("blkones", (128, 128))
    din("ident", (128, 128)); din("ltri", (128, 128))
    for L in (0, 1):
        for n in ("wq", "wk", "wv", "wo"):
            din(f"a{L}_{n}T", (D, D))
    din("rt", (128, DT))
    din("gl", (128, DT)); din("gh", (128, DT))
    din("b2l", (128, DT)); din("b2h", (128, DT))
    din("b1l", (128, 4)); din("b1h", (128, 32))
    din("w1lT", (D, 512)); din("w2lT", (512, D))
    din("w1hT_blk", (32 * 1024, 128)); din("w2hT_blk", (8 * 4096, 128))
    din("ipT_blk", (64 * 1024, 128))
    din("piwT_sh", (N_IN, N_PROC // 8))
    din("pv", (N_PROC, D_PV))
    din("oiw_shT", (D_PV, N_OUT // 8))
    din("op_sh", (N_OUT // 8, D))
    din("ohrow", (128, 8))
    din("winm", (128, 512))

    out = nc.dram_tensor("outT", [D, T], F32, kind="ExternalOutput").ap()
    Hdram = nc.dram_tensor("Hdram", [4096, T], BF16).ap()

    ccd = {}
    ccd_h = {}
    for n, shape in (("irA", (128, 512)), ("irB", (128, 512)),
                     ("paA", (8, N_PROC)), ("paB", (8, N_PROC)),
                     ("oaA", (8, N_OUT)), ("oaB", (8, N_OUT)),
                     ("osA", (8, D)), ("osB", (8, D))):
        h = nc.dram_tensor(n, list(shape), F32)
        ccd_h[n] = h
        ccd[n] = h.ap()

    X = es.enter_context(nc.sbuf_tensor("X", [128, DT * T], F32))
    MREP = es.enter_context(nc.sbuf_tensor("MREP", [128, T], F32))
    IREP = es.enter_context(nc.sbuf_tensor("IREP", [128, T], F32))
    SCB = es.enter_context(nc.sbuf_tensor("SCB", [128, T], BF16))
    ONES = es.enter_context(nc.sbuf_tensor("ONES", [128, 128], BF16))
    BLK = es.enter_context(nc.sbuf_tensor("BLK", [128, 128], BF16))
    BLKF = es.enter_context(nc.sbuf_tensor("BLKF", [128, 128], F32))
    ONE1 = es.enter_context(nc.sbuf_tensor("ONE1", [1, 64], BF16))
    ONE1F = es.enter_context(nc.sbuf_tensor("ONE1F", [1, 64], F32))
    ONESF = es.enter_context(nc.sbuf_tensor("ONESF", [128, 128], F32))
    IDN = es.enter_context(nc.sbuf_tensor("IDN", [128, 128], F32))
    LTRI = es.enter_context(nc.sbuf_tensor("LTRI", [128, 128], F32))
    SMALL = es.enter_context(nc.sbuf_tensor("SMALL", [128, 96], F32))
    SIG = es.enter_context(nc.sbuf_tensor("SIG", [128, 16], F32))
    SV = es.enter_context(nc.sbuf_tensor("SV", [128, 128], F32))

    def ap_x(t, c0=0, c1=T):
        return X[:, t * T + c0: t * T + c1]

    with nc.Block() as block:
        c.block = block
        for s in ["qX", "qX1", "qX2", "qX3", "qC", "qTAB", "qWV", "qWQK0",
                  "qWQK1", "qSH", "qMV", "qWO0", "qWO1", "qF0", "qF1", "qF2",
                  "qF2b", "qF3", "tC", "vC", "sC", "gC"]:
            c.new_sem(s)

        def loads(g):
            c.dma(g, "qC", ONES[:], dram["ones128"])
            c.dma(g, "qC", BLK[:], dram["blkones"])
            c.dma(g, "qC", ONE1[:], dram["ones128"][0:1, 0:64])
            c.dma(g, "qC", ONE1F[:], dram["ones128"][0:1, 0:64])
            c.dma(g, "qC", SMALL[:, 0:8], dram["rt"])
            c.dma(g, "qC", SMALL[:, 8:16], dram["gl"])
            c.dma(g, "qC", SMALL[:, 16:24], dram["gh"])
            c.dma(g, "qC", SMALL[:, 24:32], dram["b2l"])
            c.dma(g, "qC", SMALL[:, 32:40], dram["b2h"])
            c.dma(g, "qC", SMALL[:, 40:44], dram["b1l"])
            c.dma(g, "qC", SMALL[:, 44:76], dram["b1h"])
            c.dma(g, "qC", SMALL[0:8, 76:77], dram["onehot"])
            c.dma(g, "qC", ONESF[:], dram["ones128"])
            c.dma(g, "qC", BLKF[:], dram["blkones"])
            c.dma(g, "qC", IDN[:], dram["ident"])
            c.dma(g, "qC", LTRI[:], dram["ltri"])
        block.gpsimd(loads)
        CONST_READY = c.cnt["qC"]

        def loadx(s):
            for t in range(DT):
                c.dma(s, "qX", ap_x(t), dram["xT"][t * 128:(t + 1) * 128, :])
        block.sync(loadx)
        X_READY = c.cnt["qX"]

        def vconsts(v):
            v.memset(SV[:, 0:1], 1e-5)
            i = v.memset(SV[:, 1:2], 1.0)
            c.inc(i, "vC")
        block.vector(vconsts)
        EPS_READY = c.cnt["vC"]

        def layernorm(tag, LNX, first=False):
            def ap_lnx(t, c0=0, c1=T):
                return LNX[:, t * T + c0: t * T + c1]

            ps = contextlib.ExitStack()
            ps_sx = ps.enter_context(nc.psum_tensor(f"sx_{tag}", [128, T], F32))
            ps_s2 = ps.enter_context(nc.psum_tensor(f"s2_{tag}", [128, T], F32))

            def vcast(v):
                c.wait(v, "vC", c.cnt["vC"])
                if first:
                    c.wait(v, "qX", X_READY)
                c.wait_buf(v, "LNX")
                c.wait_buf(v, "SBUF")
                for t in range(DT):
                    i = v.tensor_copy(ap_lnx(t), ap_x(t))
                c.inc(i, "vC")
            block.vector(vcast)
            cast_done = c.cnt["vC"]

            def tsx(t_):
                c.wait(t_, "vC", cast_done)
                if first:
                    c.wait(t_, "qC", CONST_READY)
                c.wait_buf(t_, "PSUM")
                for t in range(DT):
                    for q in range(NQC):
                        i = t_.matmul(ps_sx[:, q * 512:(q + 1) * 512], ONES[:],
                                      ap_lnx(t, q * 512, (q + 1) * 512),
                                      start=(t == 0), stop=(t == DT - 1))
                c.inc(i, "tC")
            block.tensor(tsx)

            for t in range(DT):
                sq_done = c.reserve("sC")
                scb_w = c.snap("SCB")
                def ssq(sc, t=t, scb_w=scb_w):
                    c.wait(sc, "vC", cast_done)
                    c.wait_list(sc, scb_w)
                    i = sc.activation(SCB[:, :], ap_lnx(t), AF.Square)
                    c.inc_to(i, "sC")
                c.sc(ssq)

                def ts2(t_, t=t, sq_done=sq_done):
                    c.wait(t_, "sC", sq_done)
                    for q in range(NQC):
                        i = t_.matmul(ps_s2[:, q * 512:(q + 1) * 512], ONES[:],
                                      SCB[:, q * 512:(q + 1) * 512],
                                      start=(t == 0), stop=(t == DT - 1))
                    c.inc(i, "tC")
                block.tensor(ts2)
                c.use("SCB", "tC", c.cnt["tC"])
            s2_done = c.cnt["tC"]

            def vstat(v):
                c.wait(v, "tC", s2_done)
                v.tensor_scalar(MREP[:, :], ps_sx[:, :], 1.0 / D, None, ALU.mult)
                i = v.tensor_scalar(IREP[:, :], ps_s2[:, :], 1.0 / D, None, ALU.mult)
                c.fence(v, "vC", i)
                c.wait_buf(v, "SCB")
                i = v.tensor_tensor(SCB[:, :], MREP[:, :], MREP[:, :], ALU.mult)
                c.fence(v, "vC", i)
                i = v.tensor_tensor(IREP[:, :], IREP[:, :], SCB[:, :], ALU.subtract)
                c.inc(i, "vC")
            block.vector(vstat)
            var_done = c.cnt["vC"]
            c.use("PSUM", "vC", var_done)

            f1 = c.reserve("sC")
            inv_done = c.reserve("sC")
            def sstat(sc, f1=f1):
                c.wait(sc, "vC", var_done)
                i = sc.activation(IREP[:, :], IREP[:, :], AF.Ln, bias=SV[:, 0:1])
                c.fence_to(sc, "sC", i, f1)
                i = sc.activation(IREP[:, :], IREP[:, :], AF.Exp, scale=-0.5)
                c.inc_to(i, "sC")
            c.sc(sstat)

            def vapply(v):
                c.wait(v, "sC", inv_done)
                i = v.tensor_tensor(MREP[:, :], MREP[:, :], IREP[:, :], ALU.mult)
                c.fence(v, "vC", i)
                for t in range(DT):
                    i = v.tensor_tensor(SCB[:, :], ap_x(t), IREP[:, :], ALU.mult)
                    c.fence(v, "vC", i)
                    i = v.tensor_tensor(ap_lnx(t), SCB[:, :], MREP[:, :], ALU.subtract)
                    c.fence(v, "vC", i)
            block.vector(vapply)
            ln_done = c.cnt["vC"]
            c.use("SCB", "vC", ln_done)
            ps.close()
            return ln_done

        def attention(L, LNX, ln_done):
            def ap_lnx(t, c0=0, c1=T):
                return LNX[:, t * T + c0: t * T + c1]

            wqT, wkT, wvT, woT = (dram[f"a{L}_{n}T"] for n in ("wq", "wk", "wv", "wo"))

            sco = contextlib.ExitStack()
            COS = sco.enter_context(nc.sbuf_tensor(f"COS{L}", [128, T], BF16))
            SIN = sco.enter_context(nc.sbuf_tensor(f"SIN{L}", [128, T], BF16))
            VT = sco.enter_context(nc.sbuf_tensor(f"VT{L}", [128, NKT * 520], BF16))
            OUTP = sco.enter_context(nc.sbuf_tensor(f"OUTP{L}", [128, 4 * T], BF16))
            WQS = sco.enter_context(nc.sbuf_tensor(f"WQS{L}", [128, 2 * DT * 128], BF16))
            WKS = sco.enter_context(nc.sbuf_tensor(f"WKS{L}", [128, 2 * DT * 128], BF16))
            QP = sco.enter_context(nc.sbuf_tensor(f"QP{L}", [128, T], BF16))
            KP = sco.enter_context(nc.sbuf_tensor(f"KP{L}", [128, T], BF16))
            SH = sco.enter_context(nc.sbuf_tensor(f"SH{L}", [128, T], BF16))
            PT = sco.enter_context(nc.sbuf_tensor(f"PT{L}", [128, 5 * 512], BF16))
            ZR = sco.enter_context(nc.sbuf_tensor(f"ZR{L}", [1, 1024], F32))
            ZB = sco.enter_context(nc.sbuf_tensor(f"ZB{L}", [64, 512], F32))
            OSTG = sco.enter_context(nc.sbuf_tensor(f"OSTG{L}", [64, 512], BF16))
            WVB = sco.enter_context(nc.sbuf_tensor(f"WVB{L}", [128, DT * 512], BF16))
            WOB = sco.enter_context(nc.sbuf_tensor(f"WOB{L}", [128, 8 * 128], BF16))

            def gtab(g):
                c.wait_buf(g, "ROPETAB")
                c.wait_buf(g, "SBUF")
                c.dma(g, "qTAB", COS[:], dram["rope_cos"])
                c.dma(g, "qTAB", SIN[:], dram["rope_sin"])
            block.gpsimd(gtab)
            tab_ready = c.cnt["qTAB"]

            chunk_ct = 0
            for hh in range(2):
                psA = contextlib.ExitStack()
                ps_v = psA.enter_context(nc.psum_tensor(f"psv{L}{hh}", [128, 1024], F32))

                def vones(v):
                    c.wait_buf(v, "VT")
                    c.wait_buf(v, "SBUF")
                    for tt in range(NKT):
                        a3 = VT[:, tt * 520:(tt + 1) * 520].rearrange(
                            "p (h e) -> p h e", h=8)
                        i = v.memset(a3[:, :, 64:65], 1.0)
                    c.inc(i, "vC")
                block.vector(vones)
                ones_done = c.cnt["vC"]

                def gldv(g, hh=hh):
                    c.wait_buf(g, "WVB")
                    if hh == 0:
                        c.wait_buf(g, "SBUF")
                    for k in range(DT):
                        c.dma(g, "qWV", WVB[:, k * 512:(k + 1) * 512],
                              wvT[k * 128:(k + 1) * 128, hh * 512:(hh + 1) * 512])
                block.gpsimd(gldv)
                wv_ready = c.cnt["qWV"]

                for tt in range(NKT):
                    vsl = tt % 2
                    def tf(t_, tt=tt, vsl=vsl, hh=hh, wv_ready=wv_ready):
                        c.wait(t_, "qWV", wv_ready)
                        if tt == 0:
                            c.wait(t_, "vC", ln_done)
                            c.wait_buf(t_, "PSUM")
                        else:
                            c.wait_buf(t_, f"ps_v{vsl}")
                        for k in range(DT):
                            i = t_.matmul(ps_v[:, vsl * 512:(vsl + 1) * 512],
                                          ap_lnx(k, tt * 128, (tt + 1) * 128),
                                          WVB[:, k * 512:(k + 1) * 512],
                                          start=(k == 0), stop=(k == DT - 1))
                        c.inc(i, "tC")
                    block.tensor(tf)
                    mm_done = c.cnt["tC"]

                    sfv = c.reserve("sC")
                    def sf(sc, tt=tt, vsl=vsl, mm_done=mm_done, ones_done=ones_done,
                           ps_v=ps_v):
                        c.wait(sc, "tC", mm_done)
                        c.wait(sc, "vC", ones_done)
                        dst = VT[:, tt * 520:(tt + 1) * 520].rearrange(
                            "p (h e) -> p h e", h=8)[:, :, 0:64]
                        src = ps_v[:, vsl * 512:(vsl + 1) * 512].rearrange(
                            "p (h e) -> p h e", h=8)
                        i = sc.copy(dst, src)
                        c.inc_to(i, "sC")
                    c.sc(sf)
                    c.use(f"ps_v{vsl}", "sC", sfv)
                c.use("WVB", "tC", c.cnt["tC"])
                v_done = c.cnt["sC"]
                psA.close()

                psB = contextlib.ExitStack()
                ps_s = psB.enter_context(nc.psum_tensor(f"pss{L}{hh}", [128, 5 * 512], F32))
                ps_o = psB.enter_context(nc.psum_tensor(f"pso{L}{hh}", [65, 2 * 512], F32))
                ps_zb = psB.enter_context(nc.psum_tensor(f"pszb{L}{hh}", [64, 512], F32))
                ps_qk = ps_s

                for pp2 in range(4):
                    pair = 4 * hh + pp2
                    buf = pp2 % 2

                    if pp2 == 0:
                        def gldqk(g, pair=pair, buf=buf):
                            c.wait_buf(g, f"WQS{buf}")
                            c.wait(g, f"qWQK{buf}", c.cnt[f"qWQK{buf}"])
                            if pair % 4 == 0:
                                c.wait_buf(g, "SBUF")
                            for k in range(DT):
                                c.dma(g, f"qWQK{buf}", WQS[:, (buf * DT + k) * 128:(buf * DT + k + 1) * 128],
                                      wqT[k * 128:(k + 1) * 128, pair * 128:(pair + 1) * 128])
                                c.dma(g, f"qWQK{buf}", WKS[:, (buf * DT + k) * 128:(buf * DT + k + 1) * 128],
                                      wkT[k * 128:(k + 1) * 128, pair * 128:(pair + 1) * 128])
                        block.gpsimd(gldqk)
                    wqk_ready = (f"qWQK{buf}", c.cnt[f"qWQK{buf}"])

                    for which in ("q", "k"):
                        wsl = WQS if which == "q" else WKS
                        PBUF = QP if which == "q" else KP

                        def tproj(t_, which=which, wsl=wsl, buf=buf, pair=pair,
                                  wqk_ready=wqk_ready, v_done=v_done):
                            c.wait(t_, wqk_ready[0], wqk_ready[1])
                            c.wait(t_, "qTAB", tab_ready)
                            if which == "q" and pair % 4 == 0:
                                c.wait(t_, "vC", ln_done)
                                c.wait(t_, "sC", v_done)
                                c.wait_buf(t_, "PSUM")
                            c.wait_buf(t_, "ps_qk")
                            for k2 in range(4):
                                c.wait_buf(t_, f"ps_s{k2}")
                            for q in range(NQC):
                                for k in range(DT):
                                    i = t_.matmul(
                                        ps_qk[:, q * 512:(q + 1) * 512],
                                        wsl[:, (buf * DT + k) * 128:(buf * DT + k + 1) * 128],
                                        ap_lnx(k, q * 512, (q + 1) * 512),
                                        start=(k == 0), stop=(k == DT - 1))
                            c.inc(i, "tC")
                        block.tensor(tproj)
                        proj_done = c.cnt["tC"]

                        ev_done = c.reserve("sC")
                        pbw = c.snap("QP" if which == "q" else "KP")
                        def sev(sc, PBUF=PBUF, proj_done=proj_done, pbw=pbw, ps_qk=ps_qk):
                            c.wait(sc, "tC", proj_done)
                            c.wait_list(sc, pbw)
                            i = sc.copy(PBUF[:, :], ps_qk[:, 0:2048])
                            c.inc_to(i, "sC")
                        c.sc(sev)
                        c.use("ps_qk", "sC", ev_done)

                        if which == "q":
                            def gsh(g, PBUF=PBUF, ev_done=ev_done):
                                c.wait(g, "sC", ev_done)
                                c.wait_buf(g, "SH")
                                for (d0, s0) in ((0, 32), (32, 0), (64, 96), (96, 64)):
                                    c.dma(g, "qSH", SH[d0:d0 + 32, :], PBUF[s0:s0 + 32, :])
                            block.gpsimd(gsh)
                            sh_done = c.cnt["qSH"]

                            def vr(v, PBUF=PBUF, sh_done=sh_done):
                                c.wait(v, "qSH", sh_done)
                                v.tensor_tensor(PBUF[:, :], PBUF[:, :], COS[:, :], ALU.mult)
                                i = v.tensor_tensor(SH[:, :], SH[:, :], SIN[:, :], ALU.mult)
                                c.fence(v, "vC", i)
                                i = v.tensor_tensor(PBUF[:, :], PBUF[:, :], SH[:, :], ALU.add)
                                c.inc(i, "vC")
                            block.vector(vr)
                            c.use("SH", "vC", c.cnt["vC"])
                        else:
                            rope_hd = []
                            prev_shk = None
                            for hf in range(2):
                                p0 = 64 * hf
                                def gshk(g, hf=hf, p0=p0, ev_done=ev_done,
                                         prev_shk=prev_shk):
                                    if hf == 0:
                                        c.wait(g, "sC", ev_done)
                                        c.wait_buf(g, "SH")
                                    else:
                                        c.wait(g, "qSH", prev_shk)
                                    for (d0, s0) in ((0, 32), (32, 0)):
                                        c.dma(g, "qSH", SH[p0 + d0:p0 + d0 + 32, :],
                                              KP[p0 + s0:p0 + s0 + 32, :])
                                block.gpsimd(gshk)
                                shk = c.cnt["qSH"]
                                prev_shk = shk

                                def vrk(v, p0=p0, shk=shk):
                                    c.wait(v, "qSH", shk)
                                    v.tensor_tensor(KP[p0:p0 + 64, :], KP[p0:p0 + 64, :],
                                                    COS[p0:p0 + 64, :], ALU.mult)
                                    i = v.tensor_tensor(SH[p0:p0 + 64, :],
                                                        SH[p0:p0 + 64, :],
                                                        SIN[p0:p0 + 64, :], ALU.mult)
                                    c.fence(v, "vC", i)
                                    i = v.tensor_tensor(KP[p0:p0 + 64, :],
                                                        KP[p0:p0 + 64, :],
                                                        SH[p0:p0 + 64, :], ALU.add)
                                    c.inc(i, "vC")
                                block.vector(vrk)
                                rope_hd.append(c.cnt["vC"])
                            c.use("SH", "vC", c.cnt["vC"])
                    rope0_done, rope1_done = rope_hd
                    c.use(f"WQS{buf}", "tC", c.cnt["tC"])

                    if pp2 < 3:
                        def gldqk2(g, pair=pair + 1, buf=buf ^ 1):
                            c.wait_buf(g, f"WQS{buf}")
                            c.wait(g, f"qWQK{buf}", c.cnt[f"qWQK{buf}"])
                            for k in range(DT):
                                c.dma(g, f"qWQK{buf}", WQS[:, (buf * DT + k) * 128:(buf * DT + k + 1) * 128],
                                      wqT[k * 128:(k + 1) * 128, pair * 128:(pair + 1) * 128])
                                c.dma(g, f"qWQK{buf}", WKS[:, (buf * DT + k) * 128:(buf * DT + k + 1) * 128],
                                      wkT[k * 128:(k + 1) * 128, pair * 128:(pair + 1) * 128])
                        block.gpsimd(gldqk2)

                    jobs = []
                    for sub in range(2):
                        for qc in range(NQC):
                            nkt = 4 * qc + 4
                            for kt in range(nkt):
                                jobs.append((sub, qc, kt, nkt))
                    NJ = len(jobs)
                    G, DDEL = 5, 1
                    meta = {}
                    sched = []

                    def emit_ts(j, rope0_done=rope0_done, rope1_done=rope1_done):
                        sub, qc, kt, nkt = jobs[j]
                        slot = j % 5
                        pb = 64 * sub
                        def ts(t_, j=j, qc=qc, kt=kt, slot=slot, pb=pb,
                               rope0_done=rope0_done, rope1_done=rope1_done):
                            if j == 0:
                                c.wait(t_, "vC", rope0_done)
                            elif j == 40:
                                c.wait(t_, "vC", rope1_done)
                            c.wait_buf(t_, f"ps_s{slot}")
                            i = t_.matmul(ps_s[:, slot * 512:(slot + 1) * 512],
                                          KP[pb:pb + 64, kt * 128:(kt + 1) * 128],
                                          QP[pb:pb + 64, qc * 512:(qc + 1) * 512],
                                          start=True, stop=True)
                            c.inc(i, "tC")
                        block.tensor(ts)
                        s_done = c.cnt["tC"]

                        e_done = c.reserve("sC")
                        ptw = c.snap(f"PT{slot}")
                        def sexp(sc, slot=slot, s_done=s_done, ptw=ptw, ps_s=ps_s):
                            c.wait(sc, "tC", s_done)
                            c.wait_list(sc, ptw)
                            i = sc.activation(PT[:, slot * 512:(slot + 1) * 512],
                                              ps_s[:, slot * 512:(slot + 1) * 512],
                                              AF.Exp, scale=0.125)
                            c.inc_to(i, "sC")
                        c.sc(sexp)
                        c.use(f"ps_s{slot}", "sC", e_done)

                        if kt >= 4 * qc:
                            delta = 512 * qc - 128 * kt
                            def gmask(g, slot=slot, delta=delta, e_done=e_done):
                                c.wait(g, "sC", e_done)
                                i = g.affine_select(
                                    PT[:, slot * 512:(slot + 1) * 512],
                                    PT[:, slot * 512:(slot + 1) * 512],
                                    [[1, 512]], ALU.is_ge, 0.0,
                                    base=delta, channel_multiplier=-1)
                                c.inc(i, "gC")
                            block.gpsimd(gmask)
                            mg = ("gC", c.cnt["gC"])
                            c.add_use(f"PT{slot}", "gC", c.cnt["gC"])
                        else:
                            mg = ("sC", e_done)
                        meta[j] = (slot, mg)

                    def emit_tpv(j, pos, pp2=pp2):
                        nonlocal chunk_ct
                        sub, qc, kt, nkt = jobs[j]
                        slot, mg = meta[j]
                        hl = 2 * pp2 + sub
                        oslot = chunk_ct % 2
                        if kt == 0:
                            for p in [p for p in sched if p[1][2] == oslot]:
                                sched.remove(p)
                                emit_tail(p[1])
                        def tpv(t_, kt=kt, hl=hl, slot=slot, nkt=nkt, mg=mg, oslot=oslot):
                            c.wait(t_, mg[0], mg[1])
                            if kt == 0:
                                c.wait_buf(t_, f"ps_o{oslot}")
                            i = t_.matmul(
                                ps_o[:, oslot * 512:(oslot + 1) * 512],
                                VT[:, kt * 520 + hl * 65: kt * 520 + (hl + 1) * 65],
                                PT[:, slot * 512:(slot + 1) * 512],
                                start=(kt == 0), stop=(kt == nkt - 1))
                            c.inc(i, "tC")
                        block.tensor(tpv)
                        c.use(f"PT{slot}", "tC", c.cnt["tC"])
                        if kt == nkt - 1:
                            pv_done = c.cnt["tC"]
                            def vz(v, pv_done=pv_done, oslot=oslot):
                                c.wait(v, "tC", pv_done)
                                c.wait_buf(v, f"ZR{oslot}")
                                i = v.reciprocal(ZR[:, oslot * 512:(oslot + 1) * 512],
                                                 ps_o[64:65, oslot * 512:(oslot + 1) * 512])
                                c.inc(i, "vC")
                            block.vector(vz)
                            zr_done = c.cnt["vC"]
                            sched.append((pos + DDEL, (sub, qc, oslot, zr_done)))
                            chunk_ct += 1

                    def emit_tail(payload, pp2=pp2):
                        sub, qc, oslot, zr_done = payload
                        def tzb(t_, zr_done=zr_done, oslot=oslot):
                            c.wait(t_, "vC", zr_done)
                            c.wait_buf(t_, "ps_zb")
                            i = t_.matmul(ps_zb[:, :], ONE1F[:, :],
                                          ZR[:, oslot * 512:(oslot + 1) * 512],
                                          start=True, stop=True)
                            c.inc(i, "tC")
                        block.tensor(tzb)
                        zb_done = c.cnt["tC"]
                        c.use(f"ZR{oslot}", "tC", zb_done)

                        def vev(v, qc=qc, pp2=pp2, sub=sub, oslot=oslot, zb_done=zb_done):
                            c.wait(v, "tC", zb_done)
                            if sub == 1:
                                c.wait_buf(v, "OSTG")
                            i = v.tensor_copy(ZB[:, :], ps_zb[:, :])
                            c.fence(v, "vC", i)
                            if sub == 0:
                                dst = OUTP[0:64, pp2 * T + qc * 512: pp2 * T + (qc + 1) * 512]
                            else:
                                dst = OSTG[:, :]
                            i = v.tensor_tensor(dst, ps_o[0:64, oslot * 512:(oslot + 1) * 512],
                                                ZB[:, :], ALU.mult)
                            c.inc(i, "vC")
                        block.vector(vev)
                        c.use("ps_zb", "vC", c.cnt["vC"])
                        c.use(f"ps_o{oslot}", "vC", c.cnt["vC"])
                        c.use("ZB", "vC", c.cnt["vC"])
                        oev = c.cnt["vC"]

                        if sub == 1:
                            def gmv(g, qc=qc, pp2=pp2, oev=oev):
                                c.wait(g, "vC", oev)
                                c.dma(g, "qMV",
                                      OUTP[64:128, pp2 * T + qc * 512: pp2 * T + (qc + 1) * 512],
                                      OSTG[:, :])
                            block.gpsimd(gmv)
                            c.use("OSTG", "qMV", c.cnt["qMV"])

                    NG = (NJ + G - 1) // G
                    for gp in range(NG + 1):
                        for p in [p for p in sched if p[0] <= gp]:
                            sched.remove(p)
                            emit_tail(p[1])
                        if gp >= 1:
                            for j in range(G * (gp - 1), min(G * gp, NJ)):
                                emit_tpv(j, gp)
                        if gp < NG:
                            for j in range(G * gp, min(G * (gp + 1), NJ)):
                                emit_ts(j)
                    for p in sorted(sched, key=lambda t: t[0]):
                        emit_tail(p[1])
                    del sched[:]

                    c.use("QP", "tC", c.cnt["tC"])
                    c.use("KP", "tC", c.cnt["tC"])
                attn_v = c.cnt["vC"]
                attn_g = c.cnt["qMV"]
                psB.close()
                c.use("PSUM", "vC", attn_v)
                c.add_use("PSUM", "sC", c.cnt["sC"])

                psC = contextlib.ExitStack()
                ps_r = psC.enter_context(nc.psum_tensor(f"psr{L}{hh}", [128, 1024], F32))
                for mt in range(DT):
                    wsl = mt % 2
                    def gldo2(g, mt=mt, wsl=wsl, hh=hh):
                        c.wait_buf(g, f"WOB{wsl}")
                        for k in range(4):
                            c.dma(g, f"qWO{wsl}", WOB[:, (wsl * 4 + k) * 128:(wsl * 4 + k + 1) * 128],
                                  woT[hh * 512 + k * 128: hh * 512 + (k + 1) * 128,
                                      mt * 128:(mt + 1) * 128])
                    block.gpsimd(gldo2)
                    wo_ready = c.cnt[f"qWO{wsl}"]
                    for q in range(NQC):
                        rsl = (mt * NQC + q) % 2
                        def tro(t_, mt=mt, q=q, wsl=wsl, rsl=rsl, wo_ready=wo_ready):
                            c.wait(t_, f"qWO{wsl}", wo_ready)
                            if mt == 0 and q == 0:
                                c.wait(t_, "vC", attn_v)
                                c.wait(t_, "qMV", attn_g)
                                c.wait_buf(t_, "PSUM")
                            c.wait_buf(t_, f"ps_r{rsl}")
                            for k in range(4):
                                i = t_.matmul(ps_r[:, rsl * 512:(rsl + 1) * 512],
                                              WOB[:, (wsl * 4 + k) * 128:(wsl * 4 + k + 1) * 128],
                                              OUTP[:, k * T + q * 512: k * T + (q + 1) * 512],
                                              start=(k == 0), stop=(k == 3))
                            c.inc(i, "tC")
                        block.tensor(tro)
                        o_done = c.cnt["tC"]

                        def vadd(v, mt=mt, q=q, rsl=rsl, o_done=o_done):
                            c.wait(v, "tC", o_done)
                            i = v.tensor_tensor(ap_x(mt, q * 512, (q + 1) * 512),
                                                ap_x(mt, q * 512, (q + 1) * 512),
                                                ps_r[:, rsl * 512:(rsl + 1) * 512],
                                                ALU.add)
                            c.inc(i, "vC")
                        block.vector(vadd)
                        c.use(f"ps_r{rsl}", "vC", c.cnt["vC"])
                    c.use(f"WOB{wsl}", "tC", c.cnt["tC"])
                c.use("VT", "sC", c.cnt["sC"])
                c.add_use("VT", "tC", c.cnt["tC"])
                c.use("OUTP", "tC", c.cnt["tC"])
                psC.close()
                c.use("PSUM", "vC", c.cnt["vC"])
            resid = c.cnt["vC"]
            c.use("LNX", "tC", c.cnt["tC"])
            c.use("PSUM", "vC", resid)
            c.use("ROPETAB", "vC", c.cnt["vC"])
            sco.close()
            c.sbuf_epoch(extra=("qMV", "qSH"))
            return resid


        def ffn0():
            C1 = float(np.log(T / COOR_K))
            ECAP = float(np.exp(CAP))

            scA = contextlib.ExitStack()
            LNXF = scA.enter_context(nc.sbuf_tensor("LNXF", [128, DT * T], BF16))

            def ap_lnx(t, c0=0, c1=T):
                return LNXF[:, t * T + c0: t * T + c1]

            ln_done = layernorm("ln2", LNXF)

            ss = contextlib.ExitStack()
            SRO = ss.enter_context(nc.sbuf_tensor("SRO", [1, T], F32))
            psU = ss.enter_context(nc.psum_tensor("psU", [1, T], F32))
            psS = ss.enter_context(nc.psum_tensor("psS", [1, 1], F32))
            psT = ss.enter_context(nc.psum_tensor("psT", [128, 16], F32))

            def t_s(t_):
                c.wait(t_, "vC", ln_done)
                c.wait_buf(t_, "PSUM")
                for t in range(DT):
                    for q in range(NQC):
                        t_.matmul(psU[:, q * 512:(q + 1) * 512],
                                  SMALL[:, t:t + 1], ap_x(t, q * 512, (q + 1) * 512),
                                  start=(t == 0), stop=(t == DT - 1))
                for t in range(DT):
                    i = t_.matmul(psS[:, :], SMALL[:, t:t + 1], SV[:, 1:2],
                                  start=(t == 0), stop=(t == DT - 1))
                c.inc(i, "tC")
            block.tensor(t_s)
            u_done = c.cnt["tC"]

            SRO2 = ss.enter_context(nc.sbuf_tensor("SRO2", [1, T], F32))

            def v_s(v, u_done=u_done):
                c.wait(v, "tC", u_done)
                i = v.tensor_copy(SRO2[:, :], psU[:, :])
                c.fence(v, "vC", i)
                i = v.tensor_copy(SV[0:1, 3:4], psS[:, :])
                c.fence(v, "vC", i)
                i = v.tensor_tensor(SRO[:, :], psU[:, :], IREP[0:1, :], ALU.mult)
                c.fence(v, "vC", i)
                i = v.scalar_tensor_tensor(SRO[:, :], MREP[0:1, :], SV[0:1, 3:4],
                                           SRO[:, :], ALU.mult, ALU.subtract)
                c.fence(v, "vC", i)
                i = v.tensor_scalar(SRO[:, :], SRO[:, :], -1.0, None, ALU.mult)
                c.inc(i, "vC")
            block.vector(v_s)
            sro_done = c.cnt["vC"]

            def t_tr(t_, sro_done=sro_done):
                c.wait(t_, "vC", sro_done)
                for cc in range(16):
                    i = t_.transpose(psT[:, cc:cc + 1], SRO[:, cc * 128:(cc + 1) * 128],
                                     IDN[0:1, 0:1])
                c.inc(i, "tC")
            block.tensor(t_tr)
            str_done = c.cnt["tC"]

            def v_sig(v, str_done=str_done):
                c.wait(v, "tC", str_done)
                i = v.tensor_scalar(SIG[:, :], psT[:, :], -C1, CAP, ALU.add, ALU.min)
                c.inc(i, "vC")
            block.vector(v_sig)
            sg = c.cnt["vC"]

            sig_ready = c.reserve("sC")
            def s_sig(sc, sg=sg):
                c.wait(sc, "vC", sg)
                i = sc.activation(SIG[:, :], SIG[:, :], AF.Exp)
                c.inc_to(i, "sC")
            c.sc(s_sig)
            c.use("PSUM", "sC", sig_ready)
            if debug_stage == "sig":
                print("EMITTING STDBG")
                def stdbg(s):
                    c.wait(s, "sC", sig_ready)
                    c.dma(s, "qX", out[1:2, :], SRO[:, :])
                    c.dma(s, "qX", out[2:130, 0:16], SIG[:, :])
                    c.dma(s, "qX", out[130:131, :], SRO2[:, :])
                    c.dma(s, "qX", out[515:516, 48:56], IREP[0:1, 0:8])
                    c.dma(s, "qX", out[516:517, 48:56], MREP[0:1, 0:8])
                    c.wait(s, "qX", c.cnt["qX"])
                c.sc_flush()
                block.sync(stdbg)
                return ("SIGSTOP", sig_ready)
            ss.close()

            rs = contextlib.ExitStack()
            psR = rs.enter_context(nc.psum_tensor("psR", [128, T], F32))
            for t in range(DT):
                sq_done = c.reserve("sC")
                scb_w = c.snap("SCB")
                def s_sq(sc, t=t, scb_w=scb_w):
                    c.wait_list(sc, scb_w)
                    i = sc.activation(SCB[:, :], ap_lnx(t), AF.Square)
                    c.inc_to(i, "sC")
                c.sc(s_sq)

                def t_sq(t_, t=t, sq_done=sq_done):
                    c.wait(t_, "sC", sq_done)
                    if t == 0:
                        c.wait_buf(t_, "PSUM")
                    for q in range(NQC):
                        i = t_.matmul(psR[:, q * 512:(q + 1) * 512], ONES[:],
                                      SCB[:, q * 512:(q + 1) * 512],
                                      start=(t == 0), stop=(t == DT - 1))
                    c.inc(i, "tC")
                block.tensor(t_sq)
                c.use("SCB", "tC", c.cnt["tC"])
            ssq_done = c.cnt["tC"]

            f1 = c.reserve("sC")
            rr1 = c.reserve("sC")
            def s_rr(sc, ssq_done=ssq_done, f1=f1):
                c.wait(sc, "tC", ssq_done)
                i = sc.activation(MREP[:, :], psR[:, :], AF.Ln)
                c.fence_to(sc, "sC", i, f1)
                i = sc.activation(MREP[:, :], MREP[:, :], AF.Exp, scale=0.5)
                c.inc_to(i, "sC")
            c.sc(s_rr)
            c.use("PSUM", "sC", rr1)
            rs.close()

            def v_rr(v, rr1=rr1):
                c.wait(v, "sC", rr1)
                i = v.tensor_scalar(MREP[:, :], MREP[:, :], 1e-12, None, ALU.max)
                c.fence(v, "vC", i)
                i = v.reciprocal(MREP[:, :], MREP[:, :])
                c.fence(v, "vC", i)
                v.tensor_scalar(SV[:, 4:12], SMALL[:, 8:16], 32.0, None, ALU.mult)
                i = v.tensor_scalar(SV[:, 12:20], SMALL[:, 16:24], 32.0, None, ALU.mult)
                c.fence(v, "vC", i)
            block.vector(v_rr)

            scB = contextlib.ExitStack()
            HB = scB.enter_context(nc.sbuf_tensor("HB0", [128, DT * T], BF16))
            for t in range(DT):
                def v_hh(v, t=t):
                    if t == 0:
                        c.wait_buf(v, "SBUF")
                    i = v.tensor_tensor(HB[:, t * T:(t + 1) * T], ap_lnx(t),
                                        MREP[:, :], ALU.mult)
                    c.fence(v, "vC", i)
                    i = v.tensor_scalar(HB[:, t * T:(t + 1) * T],
                                        HB[:, t * T:(t + 1) * T],
                                        SV[:, 12 + t:13 + t], None, ALU.mult)
                    c.fence(v, "vC", i)
                block.vector(v_hh)
            for t in range(DT):
                def v_hl(v, t=t):
                    i = v.tensor_tensor(ap_lnx(t), ap_lnx(t), MREP[:, :], ALU.mult)
                    c.fence(v, "vC", i)
                    i = v.tensor_scalar(ap_lnx(t), ap_lnx(t),
                                        SV[:, 4 + t:5 + t], None, ALU.mult)
                    c.fence(v, "vC", i)
                block.vector(v_hl)
            h_done = c.cnt["vC"]

            ls = contextlib.ExitStack()
            HL = ls.enter_context(nc.sbuf_tensor("HL0", [128, 4 * T], BF16))
            WL = ls.enter_context(nc.sbuf_tensor("WL0", [128, DT * 512], BF16))
            WB = ls.enter_context(nc.sbuf_tensor("WB0", [128, 4 * 1024], BF16))
            psL = ls.enter_context(nc.psum_tensor("psL", [128, 2 * 512], F32))

            def g_wl(g):
                c.wait_buf(g, "SBUF")
                for k in range(DT):
                    c.dma(g, "qF0", WL[:, k * 512:(k + 1) * 512],
                          dram["w1lT"][k * 128:(k + 1) * 128, :])
                for k in range(4):
                    c.dma(g, "qF1", WB[:, k * 1024:(k + 1) * 1024],
                          dram["w2lT"][k * 128:(k + 1) * 128, :])
            block.gpsimd(g_wl)
            wl_ready = c.cnt["qF0"]
            wb_ready = c.cnt["qF1"]

            for m in range(4):
                for ch in range(NQC):
                    pp = (m * NQC + ch) % 2
                    def t_l1(t_, m=m, ch=ch, pp=pp):
                        if m == 0 and ch == 0:
                            c.wait(t_, "qF0", wl_ready)
                            c.wait(t_, "vC", h_done)
                            c.wait(t_, "sC", rr1)
                        c.wait_buf(t_, f"psL{pp}")
                        for k in range(DT):
                            i = t_.matmul(psL[:, pp * 512:(pp + 1) * 512],
                                          WL[:, k * 512 + m * 128: k * 512 + (m + 1) * 128],
                                          ap_lnx(k, ch * 512, (ch + 1) * 512),
                                          start=(k == 0), stop=(k == DT - 1))
                        c.inc(i, "tC")
                    block.tensor(t_l1)
                    mm_done = c.cnt["tC"]

                    l1v = c.reserve("sC")
                    def s_l1(sc, m=m, ch=ch, pp=pp, mm_done=mm_done):
                        c.wait(sc, "tC", mm_done)
                        i = sc.activation(HL[:, m * T + ch * 512: m * T + (ch + 1) * 512],
                                          psL[:, pp * 512:(pp + 1) * 512],
                                          AF.Gelu, bias=SMALL[:, 40 + m:41 + m])
                        c.inc_to(i, "sC")
                    c.sc(s_l1)
                    c.use(f"psL{pp}", "sC", l1v)
            l1_done = c.cnt["sC"]

            for mt in range(DT):
                for ch in range(NQC):
                    pp = (mt * NQC + ch) % 2
                    def t_l2(t_, mt=mt, ch=ch, pp=pp):
                        if mt == 0 and ch == 0:
                            c.wait(t_, "qF1", wb_ready)
                            c.wait(t_, "sC", l1_done)
                        c.wait_buf(t_, f"psL{pp}")
                        for k in range(4):
                            i = t_.matmul(psL[:, pp * 512:(pp + 1) * 512],
                                          WB[:, k * 1024 + mt * 128: k * 1024 + (mt + 1) * 128],
                                          HL[:, k * T + ch * 512: k * T + (ch + 1) * 512],
                                          start=(k == 0), stop=(k == 3))
                        c.inc(i, "tC")
                    block.tensor(t_l2)
                    mm2 = c.cnt["tC"]

                    def v_l2(v, mt=mt, ch=ch, pp=pp, mm2=mm2):
                        c.wait(v, "tC", mm2)
                        i = v.scalar_tensor_tensor(
                            ap_x(mt, ch * 512, (ch + 1) * 512),
                            psL[:, pp * 512:(pp + 1) * 512],
                            SMALL[:, 24 + mt:25 + mt],
                            ap_x(mt, ch * 512, (ch + 1) * 512),
                            ALU.add, ALU.add)
                        c.fence(v, "vC", i)
                    block.vector(v_l2)
                    c.use(f"psL{pp}", "vC", c.cnt["vC"])
            light_done = c.cnt["vC"]
            c.use("LNX", "tC", c.cnt["tC"])
            ls.close()
            c.sbuf_epoch()

            hs = contextlib.ExitStack()
            WH1 = hs.enter_context(nc.sbuf_tensor("WH1", [128, 2 * DT * 128], BF16))
            HST = hs.enter_context(nc.sbuf_tensor("HST", [128, 2 * T], BF16))
            psH = hs.enter_context(nc.psum_tensor("psH", [128, 2 * 1024], F32))
            psZ = hs.enter_context(nc.psum_tensor("psZ", [128, 1], F32))

            citer = [0]

            def emit_coor_iter():
                n = citer[0]

                def v_c1(v, n=n):
                    if n == 0:
                        c.wait(v, "sC", sig_ready)
                    c.wait_buf(v, "SIGt")
                    i = v.tensor_scalar(SV[:, 21:37], SIG[:, :], 1.0, None, ALU.min)
                    c.fence(v, "vC", i)
                    i = v.reduce_sum(SV[:, 20:21], SV[:, 21:37], mybir.AxisListType.X)
                    c.inc(i, "vC")
                block.vector(v_c1)
                part = c.cnt["vC"]

                def t_c(t_, part=part, n=n):
                    c.wait(t_, "vC", part)
                    c.wait_buf(t_, "psZ")
                    if n == 0:
                        c.wait(t_, "vC", light_done)
                    i = t_.matmul(psZ[:, :], ONESF[:, :], SV[:, 20:21],
                                  start=True, stop=True)
                    c.inc(i, "tC")
                block.tensor(t_c)
                zd = c.cnt["tC"]

                def v_c2(v, zd=zd):
                    c.wait(v, "tC", zd)
                    i = v.reciprocal(SV[:, 2:3], psZ[:, :])
                    c.fence(v, "vC", i)
                    i = v.tensor_scalar(SIG[:, :], SIG[:, :], SV[:, 2:3],
                                        float(COOR_K), ALU.mult, ALU.mult)
                    c.fence(v, "vC", i)
                    i = v.tensor_scalar(SIG[:, :], SIG[:, :], ECAP, None, ALU.min)
                    c.inc(i, "vC")
                block.vector(v_c2)
                c.use("psZ", "vC", c.cnt["vC"])
                c.use("SIGt", "vC", c.cnt["vC"])
                citer[0] += 1

            for m in range(32):
                buf = m % 2
                qw = "qF0" if buf == 0 else "qF1"

                def g_wh1(g, m=m, buf=buf, qw=qw):
                    c.wait_buf(g, f"WH1{buf}")
                    if m < 2:
                        c.wait_buf(g, "SBUF")
                    c.dma(g, qw,
                          WH1[:, buf * DT * 128:(buf + 1) * DT * 128].rearrange(
                              "p (k cc) -> p k cc", cc=128),
                          dram["w1hT_blk"][m * 1024:(m + 1) * 1024, :].rearrange(
                              "(k p) cc -> p k cc", p=128))
                block.gpsimd(g_wh1)
                wh_ready = c.cnt[qw]

                for half in range(2):
                    def t_h1(t_, m=m, half=half, buf=buf, qw=qw, wh_ready=wh_ready):
                        c.wait(t_, qw, wh_ready)
                        if m == 0 and half == 0:
                            c.wait(t_, "vC", light_done)
                        c.wait_buf(t_, f"psHh{half}")
                        for ch in range(2):
                            for k in range(DT):
                                i = t_.matmul(
                                    psH[:, half * 1024 + ch * 512: half * 1024 + (ch + 1) * 512],
                                    WH1[:, (buf * DT + k) * 128:(buf * DT + k + 1) * 128],
                                    HB[:, k * T + half * 1024 + ch * 512: k * T + half * 1024 + (ch + 1) * 512],
                                    start=(k == 0), stop=(k == DT - 1))
                        c.inc(i, "tC")
                    block.tensor(t_h1)
                    h1_done = c.cnt["tC"]

                    h1v = c.reserve("sC")
                    hst_w = c.snap(f"HSTb{m % 2}")
                    def s_h1(sc, m=m, half=half, h1_done=h1_done, hst_w=hst_w):
                        c.wait(sc, "tC", h1_done)
                        c.wait_list(sc, hst_w)
                        i = sc.activation(
                            HST[:, (m % 2) * T + half * 1024: (m % 2) * T + (half + 1) * 1024],
                            psH[:, half * 1024:(half + 1) * 1024],
                            AF.Gelu, bias=SMALL[:, 44 + m:45 + m])
                        c.inc_to(i, "sC")
                    c.sc(s_h1)
                    c.use(f"psHh{half}", "sC", h1v)
                hst_done = c.cnt["sC"]

                qh = "qF2" if m % 2 == 0 else "qF2b"

                def g_hst(g, m=m, hst_done=hst_done, qh=qh):
                    c.wait(g, "sC", hst_done)
                    c.dma(g, qh, Hdram[m * 128:(m + 1) * 128, :],
                          HST[:, (m % 2) * T:((m % 2) + 1) * T])
                block.gpsimd(g_hst)
                c.use(f"HSTb{m % 2}", qh, c.cnt[qh])

                while citer[0] < min(49, (m + 1) * 2):
                    emit_coor_iter()
            while citer[0] < 49:
                emit_coor_iter()

            def v_fs(v):
                c.wait_buf(v, "SIGt")
                i = v.tensor_scalar(SIG[:, :], SIG[:, :], 1.0, None, ALU.min)
                c.inc(i, "vC")
            block.vector(v_fs)
            sig_final = c.cnt["vC"]
            heavy1_done_t = c.cnt["tC"]
            heavy1_done_q = (c.cnt["qF2"], c.cnt["qF2b"])
            c.use("PSUM", "vC", sig_final)
            hs.close()
            scB.close()
            scA.close()
            c.sbuf_epoch(extra=("qF2", "qF2b"))

            h2 = contextlib.ExitStack()
            SFL = h2.enter_context(nc.sbuf_tensor("SFL", [1, T], F32))
            psB = h2.enter_context(nc.psum_tensor("psB", [128, T], F32))
            psC2 = h2.enter_context(nc.psum_tensor("psC2", [128, 16], F32))
            psP2 = h2.enter_context(nc.psum_tensor("psP2", [128, 16], F32))

            SIGB = IREP
            MASKB = MREP

            def t_sf(t_, sig_final=sig_final):
                c.wait(t_, "vC", sig_final)
                c.wait(t_, "sC", c.cnt["sC"])
                c.wait_buf(t_, "SBUF")
                for cc in range(16):
                    i = t_.transpose(psB[0:1, cc * 128:(cc + 1) * 128],
                                     SIG[:, cc:cc + 1], IDN[:, :])
                c.inc(i, "tC")
            block.tensor(t_sf)
            sf_done = c.cnt["tC"]

            def v_sf(v, sf_done=sf_done):
                c.wait(v, "tC", sf_done)
                c.wait_buf(v, "SBUF")
                i = v.tensor_copy(SFL[:, :], psB[0:1, 0:T])
                c.inc(i, "vC")
            block.vector(v_sf)
            sfl_done = c.cnt["vC"]

            def t_sb(t_, sfl_done=sfl_done):
                c.wait(t_, "vC", sfl_done)
                for q in range(NQC):
                    i = t_.matmul(psB[:, q * 512:(q + 1) * 512], ONESF[0:1, :],
                                  SFL[:, q * 512:(q + 1) * 512], start=True, stop=True)
                c.inc(i, "tC")
            block.tensor(t_sb)
            sb_done = c.cnt["tC"]

            def v_sb(v, sb_done=sb_done):
                c.wait(v, "tC", sb_done)
                i = v.tensor_copy(SIGB[:, :], psB[:, :])
                c.inc(i, "vC")
            block.vector(v_sb)
            sigb_done = c.cnt["vC"]

            for cc in range(16):
                def v_g(v, cc=cc, sigb_done=sigb_done):
                    if cc == 0:
                        c.wait(v, "vC", sigb_done)
                        c.wait_buf(v, "SCB")
                    i = v.tensor_scalar(SCB[:, :], SIGB[:, :], SIG[:, cc:cc + 1],
                                        None, ALU.is_gt, ALU.add,
                                        accum_out=SV[:, 37 + cc:38 + cc])
                    c.fence(v, "vC", i)
                block.vector(v_g)
            g_done = c.cnt["vC"]
            c.use("SCB", "vC", g_done)

            def v_t1(v, g_done=g_done):
                c.wait(v, "vC", g_done)
                i = v.tensor_scalar(SV[:, 53:69], SIG[:, :], 1.0, None, ALU.is_ge)
                c.inc(i, "vC")
            block.vector(v_t1)
            om_done = c.cnt["vC"]

            def t_t(t_, om_done=om_done):
                c.wait(t_, "vC", om_done)
                i = t_.matmul(psC2[:, :], ONESF[:, :], SV[:, 53:69], start=True, stop=True)
                c.inc(i, "tC")
                i = t_.matmul(psP2[:, :], LTRI[:, :], SV[:, 53:69], start=True, stop=True)
                c.inc(i, "tC")
            block.tensor(t_t)
            tp_done = c.cnt["tC"]

            def v_rank(v, tp_done=tp_done):
                c.wait(v, "tC", tp_done)
                i = v.tensor_copy(SV[:, 69:85], psC2[:, :])
                c.fence(v, "vC", i)
                a, b2 = 69, 85
                for sh in (1, 2, 4, 8):
                    i = v.tensor_copy(SV[:, b2:b2 + sh], SV[:, a:a + sh])
                    c.fence(v, "vC", i)
                    i = v.tensor_tensor(SV[:, b2 + sh:b2 + 16], SV[:, a + sh:a + 16],
                                        SV[:, a:a + 16 - sh], ALU.add)
                    c.fence(v, "vC", i)
                    a, b2 = b2, a
                i = v.tensor_tensor(SV[:, a:a + 16], SV[:, a:a + 16], psC2[:, :],
                                    ALU.subtract)
                c.fence(v, "vC", i)
                i = v.tensor_tensor(SV[:, 101:117], SV[:, a:a + 16], psP2[:, :], ALU.add)
                c.fence(v, "vC", i)
                i = v.tensor_tensor(SV[:, 101:117], SV[:, 101:117], SV[:, 53:69],
                                    ALU.mult)
                c.fence(v, "vC", i)
                i = v.tensor_tensor(SV[:, 101:117], SV[:, 101:117], SV[:, 37:53],
                                    ALU.add)
                c.fence(v, "vC", i)
                i = v.tensor_scalar(SV[:, 101:117], SV[:, 101:117], float(N_HEAVY),
                                    None, ALU.is_lt)
                c.inc(i, "vC")
            block.vector(v_rank)
            rank_done = c.cnt["vC"]

            def t_mf(t_, rank_done=rank_done):
                c.wait(t_, "vC", rank_done)
                for cc in range(16):
                    i = t_.transpose(psB[0:1, cc * 128:(cc + 1) * 128],
                                     SV[:, 101 + cc:102 + cc], IDN[:, :])
                c.inc(i, "tC")
            block.tensor(t_mf)
            mf_done = c.cnt["tC"]

            def v_mf(v, mf_done=mf_done):
                c.wait(v, "tC", mf_done)
                i = v.tensor_copy(SFL[:, :], psB[0:1, 0:T])
                c.inc(i, "vC")
            block.vector(v_mf)
            mfl_done = c.cnt["vC"]

            def t_mb(t_, mfl_done=mfl_done):
                c.wait(t_, "vC", mfl_done)
                for q in range(NQC):
                    i = t_.matmul(psB[:, q * 512:(q + 1) * 512], ONESF[0:1, :],
                                  SFL[:, q * 512:(q + 1) * 512], start=True, stop=True)
                c.inc(i, "tC")
            block.tensor(t_mb)
            mb_done = c.cnt["tC"]

            def v_mb(v, mb_done=mb_done):
                c.wait(v, "tC", mb_done)
                i = v.tensor_copy(MASKB[:, :], psB[:, :])
                c.inc(i, "vC")
            block.vector(v_mb)
            mask_done = c.cnt["vC"]
            c.use("PSUM", "vC", mask_done)
            h2.close()
            c.sbuf_epoch()

            h3 = contextlib.ExitStack()
            HCH = h3.enter_context(nc.sbuf_tensor("HCH", [128, 32 * 512], BF16))
            WH2 = h3.enter_context(nc.sbuf_tensor("WH2", [128, 8 * 32 * 128], BF16))
            TMPC = h3.enter_context(nc.sbuf_tensor("TMPC", [128, 512], F32))
            psH2 = h3.enter_context(nc.psum_tensor("psH2", [128, 8 * 512], F32))

            wv_prev = c.cnt["qWV"]
            def g_wh2(g, wv_prev=wv_prev):
                c.wait_buf(g, "SBUF")
                c.wait(g, "qWV", wv_prev)
                for mt in range(DT):
                    qn = "qF0" if mt < 4 else "qWV"
                    c.dma(g, qn,
                          WH2[:, mt * 32 * 128:(mt + 1) * 32 * 128].rearrange(
                              "p (k cc) -> p k cc", cc=128),
                          dram["w2hT_blk"][mt * 4096:(mt + 1) * 4096, :].rearrange(
                              "(k p) cc -> p k cc", p=128))
            block.gpsimd(g_wh2)
            wh2_ready = (c.cnt["qF0"], c.cnt["qWV"])

            qk_prev = (c.cnt["qWQK0"], c.cnt["qWQK1"])
            QNS = ["qF1", "qF3", "qWQK0", "qWQK1"]
            for ch in range(NQC):
              for hf in range(2):
                u = ch * 2 + hf
                sl = u % 2
                qn = QNS[u % 4]
                def g_hch(g, u=u, ch=ch, hf=hf, sl=sl, qn=qn,
                          heavy1_done_q=heavy1_done_q, qk_prev=qk_prev):
                    if u == 0:
                        c.wait(g, "qF2", heavy1_done_q[0])
                        c.wait(g, "qF2b", heavy1_done_q[1])
                        c.wait_buf(g, "SBUF")
                    if u == 2:
                        c.wait(g, "qWQK0", qk_prev[0])
                    if u == 3:
                        c.wait(g, "qWQK1", qk_prev[1])
                    c.wait_buf(g, f"HCH{sl}")
                    for k in range(16):
                        kk = hf * 16 + k
                        c.dma(g, qn, HCH[:, (sl * 16 + k) * 512:(sl * 16 + k + 1) * 512],
                              Hdram[kk * 128:(kk + 1) * 128, ch * 512:(ch + 1) * 512])
                block.gpsimd(g_hch)
                hch_rdy = (qn, c.cnt[qn])

                def t_h2(t_, u=u, ch=ch, hf=hf, sl=sl, hch_rdy=hch_rdy,
                         wh2_ready=wh2_ready, mask_done=mask_done):
                    c.wait(t_, hch_rdy[0], hch_rdy[1])
                    if u == 0:
                        c.wait(t_, "qF0", wh2_ready[0])
                        c.wait(t_, "qWV", wh2_ready[1])
                        c.wait(t_, "vC", mask_done)
                    if hf == 0:
                        c.wait_buf(t_, "psH2")
                    for k in range(16):
                        kk = hf * 16 + k
                        for mt in range(DT):
                            i = t_.matmul(
                                psH2[:, mt * 512:(mt + 1) * 512],
                                WH2[:, (mt * 32 + kk) * 128:(mt * 32 + kk + 1) * 128],
                                HCH[:, (sl * 16 + k) * 512:(sl * 16 + k + 1) * 512],
                                start=(kk == 0), stop=(kk == 31))
                    c.inc(i, "tC")
                block.tensor(t_h2)
                h2_done = c.cnt["tC"]
                c.use(f"HCH{sl}", "tC", h2_done)

                if hf == 1:
                    for mt in range(DT):
                        def v_h2(v, ch=ch, mt=mt, h2_done=h2_done):
                            if mt == 0:
                                c.wait(v, "tC", h2_done)
                            i = v.scalar_tensor_tensor(
                                TMPC[:, :], psH2[:, mt * 512:(mt + 1) * 512],
                                SMALL[:, 32 + mt:33 + mt],
                                MASKB[:, ch * 512:(ch + 1) * 512],
                                ALU.add, ALU.mult)
                            c.fence(v, "vC", i)
                            i = v.tensor_tensor(ap_x(mt, ch * 512, (ch + 1) * 512),
                                                ap_x(mt, ch * 512, (ch + 1) * 512),
                                                TMPC[:, :], ALU.add)
                            c.fence(v, "vC", i)
                        block.vector(v_h2)
                    c.use("psH2", "vC", c.cnt["vC"])
            ffn_done = c.cnt["vC"]
            c.use("PSUM", "vC", ffn_done)
            h3.close()
            c.sbuf_epoch()
            return ffn_done


        def ffn1():
            scA = contextlib.ExitStack()
            LNX4 = scA.enter_context(nc.sbuf_tensor("LNX4", [128, DT * T], BF16))

            def ap_lnx(t, c0=0, c1=T):
                return LNX4[:, t * T + c0: t * T + c1]

            ln_done = layernorm("ln4", LNX4)

            PVW = scA.enter_context(nc.sbuf_tensor("PVW", [128, 32 * 512], BF16))
            OIWB = scA.enter_context(nc.sbuf_tensor("OIWB", [128, 4 * 1024], BF16))
            OPS = scA.enter_context(nc.sbuf_tensor("OPS", [128, 8 * 1024], BF16))
            PIWB = scA.enter_context(nc.sbuf_tensor("PIWB", [128, 2 * 512], BF16))
            WIN = scA.enter_context(nc.sbuf_tensor("WIN", [128, 512], F32))
            OHR = scA.enter_context(nc.sbuf_tensor("OHR", [128, 8], F32))
            IRT = scA.enter_context(nc.sbuf_tensor("IRT", [128, 512], BF16))
            GJ = scA.enter_context(nc.sbuf_tensor("GJ", [128, T], BF16))
            WIP = scA.enter_context(nc.sbuf_tensor("WIP", [128, 2 * DT * 128], BF16))
            ZB8 = scA.enter_context(nc.sbuf_tensor("ZB8", [8, 512], F32))
            PC = scA.enter_context(nc.sbuf_tensor("PC", [8, 512], F32))
            PA = scA.enter_context(nc.sbuf_tensor("PA", [128, 256], F32))
            E2 = scA.enter_context(nc.sbuf_tensor("E2", [128, 256], F32))
            W2F = scA.enter_context(nc.sbuf_tensor("W2F", [128, 256], F32))
            TRB = scA.enter_context(nc.sbuf_tensor("TRB", [128, 256], BF16))
            AGGT = scA.enter_context(nc.sbuf_tensor("AGGT", [128, 32], BF16))
            AO = scA.enter_context(nc.sbuf_tensor("AO", [128, 512], F32))
            TRO = scA.enter_context(nc.sbuf_tensor("TRO", [128, 512], BF16))
            OSP = scA.enter_context(nc.sbuf_tensor("OSP", [8, 1024], F32))
            OSB = scA.enter_context(nc.sbuf_tensor("OSB", [8, 1024], F32))
            AOC = OSP

            def g_small(g):
                c.wait_buf(g, "SBUF")
                for k in range(32):
                    c.dma(g, "qF3", PVW[:, k * 512:(k + 1) * 512],
                          dram["pv"][k * 128:(k + 1) * 128, :])
                for k in range(4):
                    c.dma(g, "qF3", OIWB[:, k * 1024:(k + 1) * 1024],
                          dram["oiw_shT"][k * 128:(k + 1) * 128, :])
                for k in range(DT):
                    c.dma(g, "qF3", OPS[:, k * 1024:(k + 1) * 1024],
                          dram["op_sh"][k * 128:(k + 1) * 128, :])
                c.dma(g, "qF3", WIN[:, :], dram["winm"])
                c.dma(g, "qF3", OHR[:, :], dram["ohrow"])
            block.gpsimd(g_small)
            small_ready = c.cnt["qF3"]

            def v_zb(v):
                i = v.memset(ZB8[:, :], 0.0)
                c.inc(i, "vC")
            block.vector(v_zb)
            zb8_done = c.cnt["vC"]

            def g_zero(g, zb8_done=zb8_done):
                c.wait(g, "vC", zb8_done)
                c.wait(g, "qWQK0", c.cnt["qWQK0"])
                c.wait(g, "qWQK1", c.cnt["qWQK1"])
                for j in range(8):
                    c.dma(g, "qWQK0", ccd["paA"][0:8, j * 512:(j + 1) * 512],
                          ZB8[:, :])
                for j in range(16):
                    c.dma(g, "qWQK1", ccd["oaA"][0:8, j * 512:(j + 1) * 512],
                          ZB8[:, :])
            block.gpsimd(g_zero)
            zpa_done = c.cnt["qWQK0"]
            zoa_done = c.cnt["qWQK1"]

            bm = contextlib.ExitStack()
            psA4 = bm.enter_context(nc.psum_tensor("psA4", [128, 2 * T], F32))
            for n in range(64):
                buf = n % 2
                qw = "qF0" if buf == 0 else "qF1"

                def g_ip(g, n=n, buf=buf, qw=qw):
                    c.wait_buf(g, f"WIP{buf}")
                    if n < 2:
                        c.wait_buf(g, "SBUF")
                    c.dma(g, qw,
                          WIP[:, buf * DT * 128:(buf + 1) * DT * 128].rearrange(
                              "p (k cc) -> p k cc", cc=128),
                          dram["ipT_blk"][n * 1024:(n + 1) * 1024, :].rearrange(
                              "(k p) cc -> p k cc", p=128))
                block.gpsimd(g_ip)
                ip_ready = c.cnt[qw]

                def t_bm(t_, n=n, buf=buf, qw=qw, ip_ready=ip_ready):
                    c.wait(t_, qw, ip_ready)
                    if n == 0:
                        c.wait(t_, "vC", ln_done)
                        c.wait_buf(t_, "PSUM")
                    c.wait_buf(t_, f"psA4{buf}")
                    for q in range(NQC):
                        for k in range(DT):
                            i = t_.matmul(
                                psA4[:, buf * T + q * 512: buf * T + (q + 1) * 512],
                                WIP[:, (buf * DT + k) * 128:(buf * DT + k + 1) * 128],
                                ap_lnx(k, q * 512, (q + 1) * 512),
                                start=(k == 0), stop=(k == DT - 1))
                    c.inc(i, "tC")
                block.tensor(t_bm)
                bm_done = c.cnt["tC"]
                c.use(f"WIP{buf}", "tC", bm_done)

                bmv = c.reserve("sC")
                gj_w = c.snap("GJ")
                def s_bm(sc, n=n, buf=buf, bm_done=bm_done, gj_w=gj_w):
                    c.wait(sc, "tC", bm_done)
                    c.wait_list(sc, gj_w)
                    i = sc.activation(GJ[:, :],
                                      psA4[:, buf * T:(buf + 1) * T],
                                      AF.Gelu, accum_out=SV[:, 37 + n:38 + n])
                    c.inc_to(i, "sC")
                c.sc(s_bm)
                c.use(f"psA4{buf}", "sC", bmv)
                c.use("GJ", "sC", bmv)
            macc_done = c.cnt["sC"]
            c.use("PSUM", "sC", macc_done)
            bm.close()

            def bisect(BUFap, F, kval, lo0, hi0, iters, grp, tag, first_wait):
                def v_init(v):
                    c.wait(v, "vC", c.cnt["vC"])
                    c.wait(v, first_wait[0], first_wait[1])
                    i = v.memset(SV[:, 120:121], 0.5 * (lo0 + hi0))
                    c.fence(v, "vC", i)
                block.vector(v_init)

                ps_cn = contextlib.ExitStack()
                psCNT = ps_cn.enter_context(
                    nc.psum_tensor(f"psCNT{tag}", [128, 1], F32))
                for it in range(iters):
                    def v_b1(v, it=it):
                        c.wait_buf(v, "SCB")
                        i = v.tensor_scalar(SCB[:, 0:F], BUFap, SV[:, 120:121],
                                            None, ALU.is_ge, ALU.add,
                                            accum_out=SV[:, 121:122])
                        c.inc(i, "vC")
                    block.vector(v_b1)
                    pd = c.cnt["vC"]

                    def t_b(t_, pd=pd, it=it, tag=tag):
                        c.wait(t_, "vC", pd)
                        if it == 0:
                            c.wait_buf(t_, "PSUM")
                        c.wait_buf(t_, f"psCNT{tag}")
                        i = t_.matmul(psCNT[:, :], grp, SV[:, 121:122],
                                      start=True, stop=True)
                        c.inc(i, "tC")
                    block.tensor(t_b)
                    cd = c.cnt["tC"]

                    d = (hi0 - lo0) / 2.0 ** (it + 2)
                    def v_b2(v, cd=cd, d=d, tag=tag):
                        c.wait(v, "tC", cd)
                        i = v.tensor_scalar(SV[:, 124:125], psCNT[:, :],
                                            float(kval), 2.0 * d,
                                            ALU.is_ge, ALU.mult)
                        c.fence(v, "vC", i)
                        i = v.tensor_scalar(SV[:, 124:125], SV[:, 124:125],
                                            -d, None, ALU.add)
                        c.fence(v, "vC", i)
                        i = v.tensor_tensor(SV[:, 120:121], SV[:, 120:121],
                                            SV[:, 124:125], ALU.add)
                        c.fence(v, "vC", i)
                    block.vector(v_b2)
                    c.use(f"psCNT{tag}", "vC", c.cnt["vC"])
                    c.use("SCB", "vC", c.cnt["vC"])
                w = (hi0 - lo0) / 2.0 ** (iters + 1)
                def v_fin(v, w=w):
                    c.wait(v, "vC", c.cnt["vC"])
                    i = v.tensor_scalar(SV[:, 118:119], SV[:, 120:121],
                                        -1.0625 * w, None, ALU.add)
                    c.inc(i, "vC")
                block.vector(v_fin)
                th_done = c.cnt["vC"]
                c.use("PSUM", "vC", th_done)
                ps_cn.close()
                return th_done

            b1 = bisect(SV[:, 37:101], 64, N_IN // 8, -400.0, 50000.0, 36,
                        ONESF[:, :], "s1", ("sC", macc_done))

            def v_ir(v, b1=b1):
                c.wait(v, "vC", b1)
                c.wait_buf(v, "SCB")
                i = v.tensor_scalar(SCB[:, 0:64], SV[:, 37:101], SV[:, 118:119],
                                    None, ALU.is_ge)
                c.fence(v, "vC", i)
                i = v.tensor_tensor(SV[:, 37:101], SV[:, 37:101], SCB[:, 0:64],
                                    ALU.mult)
                c.fence(v, "vC", i)
                i = v.tensor_scalar(SV[:, 37:101], SV[:, 37:101], 1.0 / T,
                                    None, ALU.mult)
                c.fence(v, "vC", i)
                c.wait(v, "qF3", small_ready)
                irt3 = TRO[:, :].rearrange("p (t r) -> p t r", r=8)
                for r in range(8):
                    i = v.tensor_scalar(irt3[:, :, r], SV[:, 37:101],
                                        OHR[:, r:r + 1], None, ALU.mult)
                c.inc(i, "vC")
            block.vector(v_ir)
            irt_done = c.cnt["vC"]
            c.use("SCB", "vC", irt_done)

            def g_cc1(g, irt_done=irt_done):
                c.wait(g, "vC", irt_done)
                c.dma(g, "qF0", ccd["irA"], TRO[:, :])
                c.wait(g, "qF0", c.cnt["qF0"])
                i = g.collective_compute(
                    "AllReduce", ALU.add, replica_groups=[list(range(8))],
                    ins=[ccd["irA"]], outs=[ccd["irB"]])
                c.inc(i, "gC")
                c.wait(g, "gC", c.cnt["gC"])
                c.dma(g, "qF0", IRT[:, :], ccd["irB"])
            block.gpsimd(g_cc1)
            irt_ready = c.cnt["qF0"]

            pr = contextlib.ExitStack()
            psPR = pr.enter_context(nc.psum_tensor("psPR", [8, 512], F32))
            for k in range(64):
                buf = k % 2
                qw = "qF1" if buf == 0 else "qF2"

                def g_piw(g, k=k, buf=buf, qw=qw):
                    c.wait_buf(g, f"PIWB{buf}")
                    if k < 2:
                        c.wait(g, "qF0", irt_ready)
                    c.dma(g, qw, PIWB[:, buf * 512:(buf + 1) * 512],
                          dram["piwT_sh"][k * 128:(k + 1) * 128, :])
                block.gpsimd(g_piw)
                piw_ready = c.cnt[qw]

                def t_pr(t_, k=k, buf=buf, qw=qw, piw_ready=piw_ready,
                         irt_ready=irt_ready):
                    c.wait(t_, qw, piw_ready)
                    if k == 0:
                        c.wait(t_, "qF0", irt_ready)
                        c.wait_buf(t_, "PSUM")
                    i = t_.matmul(psPR[:, :], IRT[:, k * 8:(k + 1) * 8],
                                  PIWB[:, buf * 512:(buf + 1) * 512],
                                  start=(k == 0), stop=(k == 63))
                    c.inc(i, "tC")
                block.tensor(t_pr)
                c.use(f"PIWB{buf}", "tC", c.cnt["tC"])
            pr_done = c.cnt["tC"]

            def v_pc(v, pr_done=pr_done):
                c.wait(v, "tC", pr_done)
                i = v.tensor_copy(PC[:, :], psPR[:, :])
                c.inc(i, "vC")
            block.vector(v_pc)
            pc_done = c.cnt["vC"]
            c.use("PSUM", "vC", pc_done)
            pr.close()

            def g_cc2(g, pc_done=pc_done, zpa_done=zpa_done):
                c.wait(g, "vC", pc_done)
                c.wait(g, "qWQK0", zpa_done)
                c.wait(g, "qF1", c.cnt["qF1"])
                with g.register() as r:
                    pid = g.partition_id()
                    g.reg_alu(r, pid, 512, ALU.mult)
                    dst = bass.AP(ccd_h["paA"], r, [[N_PROC, 8], [1, 512]])
                    c.dma(g, "qF1", dst, PC[:, :])
                c.wait(g, "qF1", c.cnt["qF1"])
                i = g.collective_compute(
                    "AllReduce", ALU.add, replica_groups=[list(range(8))],
                    ins=[ccd["paA"]], outs=[ccd["paB"]])
                c.inc(i, "gC")
                c.wait(g, "gC", c.cnt["gC"])
                c.dma(g, "qF1", PA[:, :],
                      ccd["paB"].rearrange("r (g f) -> (r g) f", f=256))
            block.gpsimd(g_cc2)
            pa_ready = c.cnt["qF1"]

            pa_done = c.reserve("sC")
            def s_pa(sc, pa_ready=pa_ready):
                c.wait(sc, "qF1", pa_ready)
                i = sc.activation(PA[:, :], PA[:, :], AF.Gelu)
                c.inc_to(i, "sC")
            c.sc(s_pa)

            b2 = bisect(PA[:, :], 256, N_PROC // 8, -1.0, 64.0, 30,
                        BLKF[:, :], "s2", ("sC", pa_done))

            e2_done = c.reserve("sC")
            def s_e2(sc, b2=b2):
                c.wait(sc, "vC", b2)
                i = sc.activation(E2[:, :], PA[:, :], AF.Exp)
                c.inc_to(i, "sC")
            c.sc(s_e2)

            sm = contextlib.ExitStack()
            psSM = sm.enter_context(nc.psum_tensor("psSM", [128, 1], F32))
            psTR = sm.enter_context(nc.psum_tensor("psTR", [128, 256], F32))
            psAG = sm.enter_context(nc.psum_tensor("psAG", [128, 32], F32))

            def v_sm(v, e2_done=e2_done):
                c.wait(v, "sC", e2_done)
                c.wait_buf(v, "SCB")
                i = v.tensor_scalar(SCB[:, 0:256], PA[:, :], SV[:, 118:119],
                                    None, ALU.is_ge)
                c.fence(v, "vC", i)
                i = v.tensor_tensor(E2[:, :], E2[:, :], SCB[:, 0:256], ALU.mult)
                c.fence(v, "vC", i)
                i = v.reduce_sum(SV[:, 121:122], E2[:, :], mybir.AxisListType.X)
                c.inc(i, "vC")
            block.vector(v_sm)
            sm1 = c.cnt["vC"]
            c.use("SCB", "vC", sm1)

            def t_sm(t_, sm1=sm1):
                c.wait(t_, "vC", sm1)
                c.wait_buf(t_, "PSUM")
                i = t_.matmul(psSM[:, :], BLKF[:, :], SV[:, 121:122],
                              start=True, stop=True)
                c.inc(i, "tC")
            block.tensor(t_sm)
            sm2 = c.cnt["tC"]

            def v_w2(v, sm2=sm2):
                c.wait(v, "tC", sm2)
                i = v.reciprocal(SV[:, 123:124], psSM[:, :])
                c.fence(v, "vC", i)
                i = v.tensor_scalar(W2F[:, :], E2[:, :], SV[:, 123:124],
                                    None, ALU.mult)
                c.inc(i, "vC")
            block.vector(v_w2)
            w2_done = c.cnt["vC"]

            def t_wt(t_, w2_done=w2_done):
                c.wait(t_, "vC", w2_done)
                for fh in range(2):
                    i = t_.transpose(psTR[:, fh * 128:(fh + 1) * 128],
                                     W2F[:, fh * 128:(fh + 1) * 128], IDN[:, :])
                c.inc(i, "tC")
            block.tensor(t_wt)
            wt_done = c.cnt["tC"]

            trb_done = c.reserve("sC")
            def s_trb(sc, wt_done=wt_done):
                c.wait(sc, "tC", wt_done)
                i = sc.copy(TRB[:, :], psTR[:, :])
                c.inc_to(i, "sC")
            c.sc(s_trb)

            trr = TRB[:, :].rearrange("p (fh r g) -> p fh g r", fh=2, r=8, g=16)

            def t_ag(t_, trb_done=trb_done, small_ready=small_ready):
                c.wait(t_, "sC", trb_done)
                c.wait(t_, "qF3", small_ready)
                for mtile in range(4):
                    for pt in range(32):
                        g_ = pt // 2
                        fh = pt % 2
                        i = t_.matmul(psAG[:, mtile * 8:(mtile + 1) * 8],
                                      PVW[:, pt * 512 + mtile * 128: pt * 512 + (mtile + 1) * 128],
                                      trr[:, fh, g_, :],
                                      start=(pt == 0), stop=(pt == 31))
                c.inc(i, "tC")
            block.tensor(t_ag)
            ag_done = c.cnt["tC"]

            aggt_done = c.reserve("sC")
            def s_ag(sc, ag_done=ag_done):
                c.wait(sc, "tC", ag_done)
                i = sc.copy(AGGT[:, :], psAG[:, :])
                c.inc_to(i, "sC")
            c.sc(s_ag)
            c.use("PSUM", "sC", aggt_done)
            sm.close()

            ao = contextlib.ExitStack()
            psAO = ao.enter_context(nc.psum_tensor("psAO", [8, 1024], F32))
            def t_ao(t_, aggt_done=aggt_done):
                c.wait(t_, "sC", aggt_done)
                c.wait_buf(t_, "PSUM")
                for nch in range(2):
                    for k in range(4):
                        i = t_.matmul(psAO[:, nch * 512:(nch + 1) * 512],
                                      AGGT[:, k * 8:(k + 1) * 8],
                                      OIWB[:, k * 1024 + nch * 512: k * 1024 + (nch + 1) * 512],
                                      start=(k == 0), stop=(k == 3))
                c.inc(i, "tC")
            block.tensor(t_ao)
            ao_done = c.cnt["tC"]

            aoc_done = c.reserve("sC")
            def s_ao(sc, ao_done=ao_done):
                c.wait(sc, "tC", ao_done)
                i = sc.activation(AOC[:, :], psAO[:, :], AF.Gelu)
                c.inc_to(i, "sC")
            c.sc(s_ao)
            c.use("PSUM", "sC", aoc_done)
            ao.close()

            def g_cc3(g, aoc_done=aoc_done, zoa_done=zoa_done):
                c.wait(g, "sC", aoc_done)
                c.wait(g, "qWQK1", zoa_done)
                c.wait(g, "qF2", c.cnt["qF2"])
                with g.register() as r:
                    pid = g.partition_id()
                    g.reg_alu(r, pid, 1024, ALU.mult)
                    dst = bass.AP(ccd_h["oaA"], r, [[N_OUT, 8], [1, 1024]])
                    c.dma(g, "qF2", dst, AOC[:, :])
                c.wait(g, "qF2", c.cnt["qF2"])
                i = g.collective_compute(
                    "AllReduce", ALU.add, replica_groups=[list(range(8))],
                    ins=[ccd["oaA"]], outs=[ccd["oaB"]])
                c.inc(i, "gC")
                c.wait(g, "gC", c.cnt["gC"])
                c.dma(g, "qF2", AO[:, :],
                      ccd["oaB"].rearrange("r (g f) -> (r g) f", f=512))
            block.gpsimd(g_cc3)
            ao_ready = ("qF2", c.cnt["qF2"])

            b3 = bisect(AO[:, :], 512, N_OUT // 8, -1.0, 64.0, 30,
                        BLKF[:, :], "s3", ao_ready)

            def v_m3(v, b3=b3):
                c.wait(v, "vC", b3)
                c.wait_buf(v, "SCB")
                i = v.tensor_scalar(SCB[:, 0:512], AO[:, :], SV[:, 118:119],
                                    None, ALU.is_ge)
                c.fence(v, "vC", i)
                i = v.tensor_tensor(AO[:, :], AO[:, :], SCB[:, 0:512], ALU.mult)
                c.fence(v, "vC", i)
                i = v.tensor_tensor(AO[:, :], AO[:, :], WIN[:, :], ALU.mult)
                c.inc(i, "vC")
            block.vector(v_m3)
            m3_done = c.cnt["vC"]
            c.use("SCB", "vC", m3_done)

            fo = contextlib.ExitStack()
            psTRO = fo.enter_context(nc.psum_tensor("psTRO", [128, 512], F32))
            psOS = fo.enter_context(nc.psum_tensor("psOS", [8, 1024], F32))
            psROW = fo.enter_context(nc.psum_tensor("psROW", [1, 1024], F32))
            psORT = fo.enter_context(nc.psum_tensor("psORT", [128, 8], F32))

            def t_tro(t_, m3_done=m3_done):
                c.wait(t_, "vC", m3_done)
                c.wait_buf(t_, "PSUM")
                for fh in range(4):
                    i = t_.transpose(psTRO[:, fh * 128:(fh + 1) * 128],
                                     AO[:, fh * 128:(fh + 1) * 128], IDN[:, :])
                c.inc(i, "tC")
            block.tensor(t_tro)
            tro_done = c.cnt["tC"]

            trod = c.reserve("sC")
            def s_tro(sc, tro_done=tro_done):
                c.wait(sc, "tC", tro_done)
                i = sc.copy(TRO[:, :], psTRO[:, :])
                c.inc_to(i, "sC")
            c.sc(s_tro)

            tror = TRO[:, :].rearrange("p (fh r g) -> p fh g r", fh=4, r=8, g=16)

            def t_os(t_, trod=trod):
                c.wait(t_, "sC", trod)
                for nch in range(2):
                    for j in range(64):
                        g_ = j // 4
                        fh = j % 4
                        i = t_.matmul(psOS[:, nch * 512:(nch + 1) * 512],
                                      tror[:, fh, g_, :],
                                      OPS[:, (j % 8) * 1024 + nch * 512: (j % 8) * 1024 + (nch + 1) * 512],
                                      start=(j == 0), stop=(j == 63))
                c.inc(i, "tC")
            block.tensor(t_os)
            os_done = c.cnt["tC"]

            def v_os(v, os_done=os_done):
                c.wait(v, "tC", os_done)
                i = v.tensor_copy(OSP[:, :], psOS[:, :])
                c.inc(i, "vC")
            block.vector(v_os)
            osp_done = c.cnt["vC"]

            def g_cc4(g, osp_done=osp_done):
                c.wait(g, "vC", osp_done)
                c.dma(g, "qF3", ccd["osA"], OSP[:, :])
                c.wait(g, "qF3", c.cnt["qF3"])
                i = g.collective_compute(
                    "AllReduce", ALU.add, replica_groups=[list(range(8))],
                    ins=[ccd["osA"]], outs=[ccd["osB"]])
                c.inc(i, "gC")
                c.wait(g, "gC", c.cnt["gC"])
                c.dma(g, "qF3", OSB[:, :], ccd["osB"])
            block.gpsimd(g_cc4)
            osb_ready = c.cnt["qF3"]

            def t_row(t_, osb_ready=osb_ready):
                c.wait(t_, "qF3", osb_ready)
                for nch in range(2):
                    i = t_.matmul(psROW[:, nch * 512:(nch + 1) * 512],
                                  SMALL[0:8, 76:77],
                                  OSB[:, nch * 512:(nch + 1) * 512],
                                  start=True, stop=True)
                c.inc(i, "tC")
            block.tensor(t_row)
            row_done = c.cnt["tC"]

            def v_row(v, row_done=row_done):
                c.wait(v, "tC", row_done)
                i = v.tensor_copy(OSP[0:1, :], psROW[:, :])
                c.inc(i, "vC")
            block.vector(v_row)
            orw_done = c.cnt["vC"]

            def t_ort(t_, orw_done=orw_done):
                c.wait(t_, "vC", orw_done)
                for t in range(DT):
                    i = t_.transpose(psORT[:, t:t + 1],
                                     OSP[0:1, t * 128:(t + 1) * 128], IDN[0:1, 0:1])
                c.inc(i, "tC")
            block.tensor(t_ort)
            ort_done = c.cnt["tC"]

            fin_base = c.cnt["vC"]
            def v_fin(v, ort_done=ort_done):
                c.wait(v, "tC", ort_done)
                i = v.tensor_copy(SV[:, 104:112], psORT[:, :])
                c.fence(v, "vC", i)
                for t in range(DT):
                    i = v.tensor_scalar(ap_x(t), ap_x(t), SV[:, 104 + t:105 + t],
                                        None, ALU.add)
                    c.fence(v, "vC", i)
            block.vector(v_fin)
            fin = c.cnt["vC"]
            fin_tiles = [fin_base + 2 + t for t in range(DT)]
            c.use("PSUM", "vC", fin)
            fo.close()
            scA.close()
            c.sbuf_epoch()
            return fin, fin_tiles

        def store_x(after_v, tiles=None):
            c.sc_flush()
            qs = ["qX", "qX1", "qX2", "qX3"]
            def st(s):
                for qn in qs:
                    c.wait(s, qn, c.cnt[qn])
                if tiles is None:
                    c.wait(s, "vC", after_v)
                for t in range(DT):
                    if tiles is not None:
                        c.wait(s, "vC", tiles[t])
                    c.dma(s, qs[t % 4], out[t * 128:(t + 1) * 128, :], ap_x(t))
                for qn in qs:
                    c.wait(s, qn, c.cnt[qn])
            block.sync(st)

        lnx_scope = contextlib.ExitStack()
        LNX0 = lnx_scope.enter_context(nc.sbuf_tensor("LNX0", [128, DT * T], BF16))
        ln1 = layernorm("ln1", LNX0, first=True)
        r1 = attention(0, LNX0, ln1)
        lnx_scope.close()

        if debug_stage == "x1":
            store_x(r1)

            return nc, dram

        r2 = ffn0()
        if debug_stage == "sig":
            nc._keep = c
            return nc, dram
        if debug_stage == "x2":
            store_x(r2)

            return nc, dram

        lnx_scope2 = contextlib.ExitStack()
        LNX1 = lnx_scope2.enter_context(nc.sbuf_tensor("LNX1", [128, DT * T], BF16))
        ln3 = layernorm("ln3", LNX1)
        r3 = attention(1, LNX1, ln3)
        lnx_scope2.close()

        if debug_stage == "x3":
            store_x(r3)

            return nc, dram

        r4, r4_tiles = ffn1()
        store_x(r4, tiles=r4_tiles)


    return nc, dram



def rope_tables():
    inv = (1.0 / 10000.0 ** (np.arange(0, HD, 2, dtype=np.float64) / HD))
    t = np.arange(T, dtype=np.float64)
    ang = np.outer(inv, t)
    cos32 = np.cos(ang); sin32 = np.sin(ang)
    COS = np.empty((128, T), dtype=np.float32)
    SIN = np.empty((128, T), dtype=np.float32)
    for p_ in range(128):
        j = p_ % 32
        COS[p_] = cos32[j]
        SIN[p_] = (sin32[j] if (p_ % 64) >= 32 else -sin32[j])
    return COS, SIN


def deinterleave_rows(w):
    perm = np.empty(D, dtype=np.int64)
    for h in range(H):
        b0 = h * HD
        perm[b0:b0 + 32] = b0 + np.arange(0, HD, 2)
        perm[b0 + 32:b0 + 64] = b0 + np.arange(1, HD, 2)
    return w[perm]


def make_inputs(inp, core):
    x = np.ascontiguousarray(np.asarray(inp["L0_attn_x"])[core].T)
    COS, SIN = rope_tables()
    onehot = np.zeros((8, 1), np.float32); onehot[core, 0] = 1.0
    ident = np.eye(128, dtype=np.float32)
    ltri = np.triu(np.ones((128, 128), np.float32), 1)
    winm = np.zeros((128, 512), np.float32)
    for p_ in range(128):
        g_ = p_ % 16
        lo_, hi_ = core * 1024, (core + 1) * 1024
        n0 = 512 * g_
        s0, s1 = max(lo_ - n0, 0), min(hi_ - n0, 512)
        if s0 < s1:
            winm[p_, s0:s1] = 1.0
    ones128 = np.ones((128, 128), np.float32)
    blk = np.zeros((128, 128), np.float32)
    for g in range(8):
        blk[g * 16:(g + 1) * 16, g * 16:(g + 1) * 16] = 1.0

    def vf(vname):
        v = np.asarray(inp[vname])
        return np.ascontiguousarray(v.reshape(-1, 128).T)

    a = np.asarray
    d = {
        "xT": x, "rope_cos": COS, "rope_sin": SIN,
        "onehot": onehot, "ones128": ones128, "blkones": blk,
        "ident": ident, "ltri": ltri,
        "a0_wqT": a(inp["L0_attn_wq"]).T, "a0_wkT": a(inp["L0_attn_wk"]).T,
        "a0_wvT": a(inp["L0_attn_wv"]).T, "a0_woT": a(inp["L0_attn_wo"]).T,
        "a1_wqT": deinterleave_rows(a(inp["L1_attn_Wq"])).T,
        "a1_wkT": deinterleave_rows(a(inp["L1_attn_Wk"])).T,
        "a1_wvT": a(inp["L1_attn_Wv"]).T, "a1_woT": a(inp["L1_attn_Wo"]).T,
        "rt": vf("L0_ffn_routing_token"),
        "gl": vf("L0_ffn_gamma_l"), "gh": vf("L0_ffn_gamma_h"),
        "b2l": vf("L0_ffn_b2_l"), "b2h": vf("L0_ffn_b2_h"),
        "b1l": vf("L0_ffn_b1_l"), "b1h": vf("L0_ffn_b1_h"),
        "w1lT": a(inp["L0_ffn_w1_l"]).T, "w2lT": a(inp["L0_ffn_w2_l"]).T,
        "w1hT_blk": a(inp["L0_ffn_w1_h"]).T.reshape(D, 32, 128).transpose(1, 0, 2).reshape(32 * D, 128),
        "w2hT_blk": a(inp["L0_ffn_w2_h"]).T.reshape(4096, 8, 128).transpose(1, 0, 2).reshape(8 * 4096, 128),
        "ipT_blk": a(inp["L1_ffn_input_patterns"]).T.reshape(D, 64, 128).transpose(1, 0, 2).reshape(64 * D, 128),
        "piwT_sh": a(inp["L1_ffn_process_input_weights"]).T[:, core * 512:(core + 1) * 512],
        "pv": a(inp["L1_ffn_process_values"]),
        "oiw_shT": a(inp["L1_ffn_output_input_weights"])[core * 1024:(core + 1) * 1024, :].T,
        "op_sh": a(inp["L1_ffn_output_patterns"])[core * 1024:(core + 1) * 1024, :],
        "ohrow": np.tile(onehot.reshape(1, 8), (128, 1)),
        "winm": winm,
    }
    return {k: np.ascontiguousarray(v, dtype=np.float32) for k, v in d.items()}


_NC_CACHE = {}


def kernel(**inputs):
    key = DEBUG_STAGE
    if key not in _NC_CACHE:
        _NC_CACHE[key] = build_program(debug_stage=DEBUG_STAGE)
    nc, _ = _NC_CACHE[key]
    in_maps = [make_inputs(inputs, core) for core in range(8)]
    res = run_bass_kernel_spmd(nc, in_maps, core_ids=list(range(8)))
    outv = np.empty((B, T, D), dtype=np.float32)
    for core in range(8):
        outv[core] = res.results[core]["outT"].T
    return outv
